# version 03
# baseline (speedup 1.0000x reference)
import numpy as np
import ml_dtypes
import concourse.bass as bass
import concourse.mybir as mybir
from concourse.bass import IndirectOffsetOnAxis
from concourse.bass_utils import run_bass_kernel_spmd

F32 = mybir.dt.float32
BF16 = mybir.dt.bfloat16
I32 = mybir.dt.int32
AF = mybir.ActivationFunctionType
ALU = mybir.AluOpType
AX = mybir.AxisListType

NCORES = 8
S = 1024
D = 4096
EPS = 1e-5
VIS = 614
SEQ = 2048

SB_BASE = 16512
OFF_A = 0
OFF_B = 65536
OFF_WPAN = 131072
OFF_COS = 147456
OFF_GRAW = 155648
OFF_E = 172032
OFF_F = 204800
SB_LIMIT = 212863


class Ctr:
    __slots__ = ("sem", "n", "name")

    def __init__(self, nc, name):
        self.sem = nc.alloc_semaphore(name)
        self.n = 0
        self.name = name


class Plan:
    def __init__(self):
        self.ops = []
        self.self_ctr = {}
        self.phase = "init"

    def dma(self, eng, ctr, fn, waits=()):
        w = [(c, c.n if v is None else v) for (c, v) in waits]
        self.ops.append((eng, fn, w, [(ctr, 16)], self.phase))
        ctr.n += 16

    def c(self, eng, ctr, fn, waits=(), inc=1):
        w = [(c, c.n if v is None else v) for (c, v) in waits]
        sc = self.self_ctr.get(eng)
        if sc is not None and sc.n > 0:
            w.append((sc, sc.n))
        incs = [(ctr, inc)] if ctr is not None else []
        self.ops.append((eng, fn, w, incs, self.phase))
        if ctr is not None:
            ctr.n += inc

    def raw(self, eng, fn):
        self.ops.append((eng, fn, [], [], self.phase))

    def replay(self, block):
        engines = {"sync": block.sync, "gpsimd": block.gpsimd,
                   "scalar": block.scalar, "vector": block.vector,
                   "tensor": block.tensor}

        nc = self.nc

        def emit_for(name):
            def go(eng):
                hwm = {}
                cur = [None]
                for (en, fn, waits, incs, tag) in self.ops:
                    if en != name:
                        continue
                    if nc is not None and tag != cur[0]:
                        if cur[0] is not None:
                            nc.leave_named_scope(cur[0][0], cur[0][1], False)
                        sid = nc.enter_named_scope(tag, False)[0]
                        cur[0] = (tag, sid)
                    for (c, v) in waits:
                        if v > 0 and hwm.get(id(c), -1) < v:
                            eng.wait_ge(c.sem, v)
                            hwm[id(c)] = v
                    r = fn(eng)
                    if incs:
                        insts = r if isinstance(r, list) else [r]
                        for (c, k) in incs:
                            insts[-1].then_inc(c.sem, k)
                if nc is not None and cur[0] is not None:
                    nc.leave_named_scope(cur[0][0], cur[0][1], False)
            return go

        for name, deco in engines.items():
            deco(emit_for(name))


def W(ctr, v=None):
    return (ctr, v)


def _prep_weights(Wq, Wo, w1, w2):
    def tile2(w):
        return np.ascontiguousarray(
            w.reshape(32, 128, 32, 128).transpose(2, 1, 0, 3).reshape(32, 128, 4096))

    def tile3(w):
        return np.ascontiguousarray(
            w.reshape(3, 32, 128, 32, 128).transpose(0, 3, 2, 1, 4).reshape(3, 32, 128, 4096))

    return tile2(Wq), tile2(Wo), tile3(w1), tile3(w2)


def _rope_tables(half):
    inv = 1.0 / (10000.0 ** (np.arange(0, 64, dtype=np.float64) / 64.0))
    d = np.arange(128)
    pos = half * 1024 + np.arange(1024, dtype=np.float64)
    ang = pos[None, :] * inv[d % 64][:, None]
    sc = 1.0 / np.sqrt(128.0)
    return (np.cos(ang) * sc).astype(np.float32), (np.sin(ang) * sc).astype(np.float32)


def _masks(half):
    g = half * 1024 + (np.arange(8)[None, :] * 128 + np.arange(128)[:, None])
    return (g < VIS).astype(np.float32), (g >= VIS).astype(np.float32)


def build_program(dbg=(), stop_after=None):
    nc = bass.Bass("TRN2", target_bir_lowering=False, debug=False)

    x_h = nc.dram_tensor("x", [S, D], F32, kind="ExternalInput")
    wq_h = nc.dram_tensor("wq", [32, 128, 4096], F32, kind="ExternalInput")
    wo_h = nc.dram_tensor("wo", [32, 128, 4096], F32, kind="ExternalInput")
    w1_h = nc.dram_tensor("w1", [3, 32, 128, 4096], F32, kind="ExternalInput")
    w2_h = nc.dram_tensor("w2", [3, 32, 128, 4096], F32, kind="ExternalInput")
    kv_h = nc.dram_tensor("kv", [1024, 2, 8, 16, 128], F32, kind="ExternalInput")
    bt_h = nc.dram_tensor("bt", [512], I32, kind="ExternalInput")
    gw_h = nc.dram_tensor("gw", [32, 128, 3], F32, kind="ExternalInput")
    gwl_h = nc.dram_tensor("gwl", [1, 4], F32, kind="ExternalInput")
    gb_h = nc.dram_tensor("gb", [1, 4], F32, kind="ExternalInput")
    b1t_h = nc.dram_tensor("b1t", [128, 96], F32, kind="ExternalInput")
    b2t_h = nc.dram_tensor("b2t", [128, 96], F32, kind="ExternalInput")
    cos_h = nc.dram_tensor("cost", [128, 1024], F32, kind="ExternalInput")
    sin_h = nc.dram_tensor("sint", [128, 1024], F32, kind="ExternalInput")
    mv_h = nc.dram_tensor("maskv", [128, 8], F32, kind="ExternalInput")
    mt_h = nc.dram_tensor("maskt", [128, 8], F32, kind="ExternalInput")
    par_h = nc.dram_tensor("par", [1, 1], I32, kind="ExternalInput")
    parf_h = nc.dram_tensor("parf", [1, 1], F32, kind="ExternalInput")
    tag_h = nc.dram_tensor("tag", [1, 1], I32, kind="ExternalInput")
    out_h = nc.dram_tensor("out", [S, D], F32, kind="ExternalOutput")

    shd_h = nc.dram_tensor("shd", [2, 8], F32, kind="Internal", addr_space="Shared")
    sht_h = nc.dram_tensor("sht", [2, 1], I32, kind="Internal", addr_space="Shared")

    x1scr = nc.dram_tensor("x1scr", [S, D], BF16, kind="Internal")
    hscr = nc.dram_tensor("hscr", [S, D], BF16, kind="Internal")
    x2scr = nc.dram_tensor("x2scr", [S, D], F32, kind="Internal")
    moescr = nc.dram_tensor("moescr", [D, S], F32, kind="Internal")
    kvlK = [nc.dram_tensor(f"kvlK{i}", [8192, 128], BF16, kind="Internal") for i in range(2)]
    kvlV = [nc.dram_tensor(f"kvlV{i}", [8192, 128], BF16, kind="Internal") for i in range(2)]

    dbg_h = {name: nc.dram_tensor(name, shape, F32, kind="ExternalOutput")
             for (name, shape) in dbg}

    eye16_h = nc.inline_tensor(np.eye(128, dtype=ml_dtypes.bfloat16), "eye16c")
    eye32_h = nc.inline_tensor(np.eye(128, dtype=np.float32), "eye32c")
    ones16_h = nc.inline_tensor(np.ones((1, 128), dtype=ml_dtypes.bfloat16), "ones16c")
    ones32_h = nc.inline_tensor(np.ones((1, 128), dtype=np.float32), "ones32c")

    def sb(name, shape, dtype, off):
        return nc.alloc_sbuf_tensor_at(name, shape, dtype, offset=SB_BASE + off)

    qctx = sb("qctx", [128, 32 * 1024], BF16, OFF_A)
    x1T = sb("x1T", [128, 32 * 1024], BF16, OFF_B)
    KT = [sb(f"KT{i}", [128, 8192], BF16, OFF_B + i * 32768) for i in range(2)]
    Vb = [sb(f"Vb{i}", [128, 8192], BF16, OFF_B + 16384 + i * 32768) for i in range(2)]
    attnTM = [sb(f"attnTM{j}", [128, 4096], F32, OFF_B + j * 16384) for j in range(4)]
    attnTMall = sb("attnTMall", [128, 16384], F32, OFF_B)
    shT = sb("shT", [128, 16384], BF16, OFF_B)
    act = [sb(f"act{e}", [128, 16384], BF16,
              [OFF_A, OFF_A + 32768, OFF_B + 32768][e]) for e in range(3)]
    smtm = [sb(f"smtm{j}", [128, 4096], F32, OFF_A + j * 16384) for j in range(4)]
    smtmall = sb("smtmall", [128, 16384], F32, OFF_A)

    wpan = [sb(f"wpan{i}", [128, 4096], BF16, OFF_WPAN + i * 8192) for i in range(2)]
    x1out = sb("x1out", [128, 4096], BF16, OFF_WPAN)
    h16t = sb("h16t", [128, 4096], BF16, OFF_GRAW)
    cosb = sb("cosb", [128, 1024], F32, OFF_COS)
    sinb = sb("sinb", [128, 1024], F32, OFF_COS + 4096)
    swoc = [sb(f"swoc{i}", [128, 512], F32, OFF_COS + i * 2048) for i in range(2)]
    graw = sb("graw", [128, 8192], BF16, OFF_GRAW)

    xt = [sb(f"xt{i}", [128, 4096], F32, OFF_E + i * 16384) for i in range(2)]
    ropet = sb("ropet", [128, 1024], F32, OFF_E)
    rope1 = sb("rope1", [128, 1024], F32, OFF_E + 4096)
    ptile = [sb(f"ptile{i}", [128, 1024], BF16, OFF_E + i * 2048) for i in range(2)]
    sumV = [sb(f"sumV{i}", [128, 1024], F32, [OFF_E + 4096, OFF_E + 16384][i]) for i in range(2)]
    sumO = [sb(f"sumO{i}", [128, 1024], F32, [OFF_E + 12288, OFF_E + 20480][i]) for i in range(2)]
    sbcast = sb("sbcast", [128, 1024], F32, OFF_E + 8192)
    xtile = sb("xtile", [128, 4096], F32, OFF_E)
    htile = sb("htile", [128, 4096], F32, OFF_E + 16384)
    sce = [sb(f"sce{e}", [128, 512], F32, OFF_E + e * 2048) for e in range(3)]
    smoec = sb("smoec", [128, 512], F32, OFF_E + 6144)
    mstg = sb("mstg", [128, 512], F32, OFF_E + 8192)
    sactf = sb("sactf", [128, 512], F32, OFF_E + 10240)
    a0bc = sb("a0bc", [128, 512], F32, OFF_E + 12288)
    a1bc = sb("a1bc", [128, 512], F32, OFF_E + 14336)
    x2t = sb("x2t", [128, 4096], F32, OFF_E + 16384)

    fo = [OFF_F]

    def falloc(name, shape, dtype):
        t = sb(name, shape, dtype, fo[0])
        bpp = int(np.prod(shape[1:])) * (2 if dtype == BF16 else 4)
        fo[0] += (bpp + 63) // 64 * 64
        return t

    eye16 = falloc("eye16", [128, 128], BF16)
    eye32 = falloc("eye32", [128, 128], F32)
    ones16 = falloc("ones16", [1, 128], BF16)
    ones32 = falloc("ones32", [1, 128], F32)
    sgw = falloc("sgw", [128, 96], BF16)
    sb1t = falloc("sb1t", [128, 96], F32)
    sb2t = falloc("sb2t", [128, 96], F32)
    b2c = [falloc(f"b2c{i}", [128, 32], F32) for i in range(2)]
    hbc = [falloc(f"hbc{i}", [128, 4], F32) for i in range(2)]
    smv = falloc("smv", [128, 8], F32)
    smt = falloc("smt", [128, 8], F32)
    shs = falloc("shs", [128, 8], F32)
    sbt = falloc("sbt", [128, 4], I32)
    sidxK = falloc("sidxK", [128, 32], I32)
    sidxV = falloc("sidxV", [128, 32], I32)
    sbn = falloc("sbn", [128, 96], F32)
    sag = falloc("sag", [128, 2], F32)
    snm = falloc("snm", [128, 1], F32)
    srs = falloc("srs", [128, 1], F32)
    srs2 = falloc("srs2", [128, 1], F32)
    ssums = falloc("ssums", [128, 8], F32)
    srec = falloc("srec", [128, 8], F32)
    stt1 = falloc("stt1", [128, 8], F32)
    sgt = [falloc(f"sgt{j}", [128, 4], F32) for j in range(4)]
    sa0 = falloc("sa0", [128, 1], F32)
    sa1 = falloc("sa1", [128, 1], F32)
    sgx = falloc("sgx", [128, 4], F32)
    sgs1 = falloc("sgs1", [128, 1], F32)
    sgs2 = falloc("sgs2", [128, 1], F32)
    sgs3 = falloc("sgs3", [128, 1], F32)
    srecro = sb("srecro", [1, 1024], F32, OFF_COS)
    a0row = sb("a0row", [1, 512], F32, OFF_COS)
    a1row = sb("a1row", [1, 512], F32, OFF_COS + 2048)
    sglog = sb("sglog", [4, 512], F32, OFF_COS + 4096)
    spay = falloc("spay", [1, 8], F32)
    spart = falloc("spart", [1, 8], F32)
    srow1 = falloc("srow1", [1, 8], F32)
    srow2 = falloc("srow2", [1, 8], F32)
    sg0 = falloc("sg0", [1, 4], F32)
    scv = falloc("scv", [1, 4], F32)
    sh0 = falloc("sh0", [1, 4], F32)
    sh1 = falloc("sh1", [1, 4], F32)
    scvb = falloc("scvb", [128, 4], F32)
    sgwl = falloc("sgwl", [1, 4], F32)
    sgbr = falloc("sgbr", [1, 4], F32)
    spar = falloc("spar", [1, 1], I32)
    sparf = falloc("sparf", [1, 1], F32)
    stag = falloc("stag", [1, 1], I32)
    stagp = falloc("stagp", [1, 1], I32)
    assert fo[0] < SB_LIMIT, fo[0]

    PS = nc.alloc_psum_tensor("PS", [128, 4096], F32)

    def ps(a, b):
        return PS[:, a:b]

    P = Plan()
    P.nc = nc
    dMISC = Ctr(nc, "dMISC")
    dGM = Ctr(nc, "dGM")
    dX = [Ctr(nc, "dX0"), Ctr(nc, "dX1")]
    dXW = Ctr(nc, "dXW")
    dT = Ctr(nc, "dT")
    dS = [Ctr(nc, "dS0"), Ctr(nc, "dS1")]
    dKV = Ctr(nc, "dKV")
    dW = [Ctr(nc, "dW0"), Ctr(nc, "dW1")]
    dXT = Ctr(nc, "dXT")
    dX2W = Ctr(nc, "dX2W")
    dHW = Ctr(nc, "dHW")
    dMW = Ctr(nc, "dMW")
    dMS = Ctr(nc, "dMS")
    dX2 = Ctr(nc, "dX2")
    dOUT = Ctr(nc, "dOUT")
    cPE = Ctr(nc, "cPE")
    cSC = Ctr(nc, "cSC")
    cVE = Ctr(nc, "cVE")
    cGP = Ctr(nc, "cGP")
    P.self_ctr = {"vector": cVE, "scalar": cSC, "gpsimd": cGP}
    swx = Ctr(nc, "swx")
    sdump = Ctr(nc, "sdump")
    spx = Ctr(nc, "spx")

    slot_pe = {0: 0, 1: 0}

    def dbg_dump(name, ap_fn, waits):
        if name in dbg_h:
            h = dbg_h[name]
            P.dma("gpsimd", dGM, lambda e, h=h, f=ap_fn: e.dma_start(out=h.ap(), in_=f()), waits)

    P.phase = 'p0'
    for fn in [
        lambda e: e.dma_start(out=sbt[:, :], in_=bt_h.ap().rearrange("(g p) -> p g", p=128), allow_slow_non_contiguous=True),
        lambda e: e.dma_start(out=eye16[:, :], in_=eye16_h.ap()),
        lambda e: e.dma_start(out=eye32[:, :], in_=eye32_h.ap()),
        lambda e: e.dma_start(out=ones16[:, :], in_=ones16_h.ap()),
        lambda e: e.dma_start(out=ones32[:, :], in_=ones32_h.ap()),
        lambda e: e.dma_start(out=sb1t[:, :], in_=b1t_h.ap()),
        lambda e: e.dma_start(out=sb2t[:, :], in_=b2t_h.ap()),
        lambda e: e.dma_start(out=cosb[:, :], in_=cos_h.ap()),
        lambda e: e.dma_start(out=sinb[:, :], in_=sin_h.ap()),
        lambda e: e.dma_start(out=smv[:, :], in_=mv_h.ap()),
        lambda e: e.dma_start(out=smt[:, :], in_=mt_h.ap()),
        lambda e: e.dma_start(out=sgwl[:, :], in_=gwl_h.ap()),
        lambda e: e.dma_start(out=sgbr[:, :], in_=gb_h.ap()),
        lambda e: e.dma_start(out=spar[:, :], in_=par_h.ap()),
        lambda e: e.dma_start(out=sparf[:, :], in_=parf_h.ap()),
        lambda e: e.dma_start(out=stag[:, :], in_=tag_h.ap()),
    ]:
        P.dma("sync", dMISC, fn)
    P.dma("gpsimd", dGM, lambda e: e.dma_start(out=sgw[:, :].rearrange("p (g e) -> p g e", e=3), in_=gw_h.ap().rearrange("g p e -> p g e")))
    p0w = dMISC.n
    p0g = dGM.n
    for kvh in range(8):
        P.c("vector", cVE, lambda e, k=kvh: e.tensor_scalar(
            sidxK[:, k * 4:(k + 1) * 4], sbt[:, :], 16, k, ALU.mult, ALU.add),
            waits=[W(dMISC, p0w)])
        P.c("vector", cVE, lambda e, k=kvh: e.tensor_scalar(
            sidxV[:, k * 4:(k + 1) * 4], sbt[:, :], 16, 8 + k, ALU.mult, ALU.add))
    idx_done = cVE.n

    def ln_stats(src_ap, waits):
        def f1(e):
            src = src_ap()
            r = None
            for i in range(8):
                r = e.bn_stats(sbn[:, i * 6:i * 6 + 6], src[:, i * 512:(i + 1) * 512])
            return r
        P.c("vector", cVE, f1, waits=waits)
        P.c("vector", cVE, lambda e: e.bn_aggr(sag[:, :], sbn[:, 0:48]))
        def f2(e):
            e.tensor_scalar(srs[:, :], sag[:, 1:2], EPS, None, ALU.add)
            return e.tensor_scalar(snm[:, :], sag[:, 0:1], -1.0, None, ALU.mult)
        P.c("vector", cVE, f2)
        P.c("scalar", cSC, lambda e: e.activation(srs2[:, :], srs[:, :], AF.Sqrt),
            waits=[W(cVE, None)])
        P.c("vector", cVE, lambda e: e.reciprocal(srs[:, :], srs2[:, :]), waits=[W(cSC, None)])

    P.phase = 'p1a'
    ve_x1 = []
    x1w = []
    for st in range(8):
        buf = xt[st % 2]
        dc = dX[st % 2]
        w = [W(cVE, ve_x1[st - 2])] if st >= 2 else []
        P.dma("sync", dc, lambda e, b=buf, s=st: e.dma_start(
            out=b[:, :], in_=x_h.ap()[s * 128:(s + 1) * 128, :]), w)
        ln_stats((lambda b=buf: b[:, :]), waits=[W(dc, None)])
        w = [W(cVE, None)]
        if st >= 1:
            w.append(W(dXW, x1w[st - 1]))
        P.c("vector", cVE, lambda e, b=buf: e.tensor_scalar(
            x1out[:, :], b[:, :], snm[:, 0:1], srs[:, 0:1], ALU.add, ALU.mult), waits=w)
        ve_x1.append(cVE.n)
        P.dma("sync", dXW, lambda e, s=st: e.dma_start(
            out=x1scr.ap()[s * 128:(s + 1) * 128, :], in_=x1out[:, :]), [W(cVE, None)])
        x1w.append(dXW.n)
    x1w_done = dXW.n
    dbg_dump("dbg_x1", lambda: x1scr.ap()[0:128, :], [W(dXW, x1w_done)])

    if stop_after == "p1a":
        for st in range(8):
            P.dma("gpsimd", dGM, lambda e, s=st: e.dma_start(
                out=out_h.ap()[s * 128:(s + 1) * 128, :],
                in_=x1scr.ap()[s * 128:(s + 1) * 128, :]), [W(dXW, x1w_done)])
        P.raw("gpsimd", lambda e: e.wait_ge(dGM.sem, dGM.n))
        with nc.Block() as block:
            P.replay(block)
        return nc

    P.phase = 'p1b'
    for fc in range(32):
        P.dma("sync", dT, lambda e, f=fc: e.dma_start(
            out=x1T[:, f * 1024:(f + 1) * 1024],
            in_=x1scr.ap()[:, f * 128:(f + 1) * 128], transpose=True),
            [W(dXW, x1w_done)])
    x1T_done = dT.n
    dbg_dump("dbg_x1T", lambda: x1T[:, 0:1024], [W(dT, x1T_done)])

    pe_q = []
    ve_rope = []
    sc_rope = []
    for qi in range(32):
        slot = qi % 2
        wq_w = [W(cPE, slot_pe[slot])]
        if qi < 2:
            wq_w.append(W(dXW, x1w_done))
        P.dma("gpsimd", dW[slot], lambda e, s=slot, q=qi: e.dma_start(
            out=wpan[s][:, :], in_=wq_h.ap()[q, :, :]), wq_w)

        def qmms(e, s=slot, q=qi):
            r = []
            b = s * 1024
            for fi in range(32):
                for hh in range(2):
                    r.append(e.matmul(
                        ps(b + hh * 512, b + hh * 512 + 512),
                        wpan[s][:, fi * 128:(fi + 1) * 128],
                        x1T[:, fi * 1024 + hh * 512: fi * 1024 + hh * 512 + 512],
                        start=(fi == 0), stop=(fi == 31)))
            return r
        mmw = [W(dW[slot], None), W(dT, x1T_done)]
        if qi >= 2:
            mmw += [W(cVE, ve_rope[qi - 2]), W(cSC, sc_rope[qi - 2])]
        P.c("tensor", cPE, qmms, waits=mmw)
        pe_q.append(cPE.n)
        slot_pe[slot] = cPE.n
        b = slot * 1024

        def ropes(e, b=b):
            e.mul(ropet[0:64, :], PS[64:128, b:b + 1024], -1.0)
            return e.copy(ropet[64:128, :], PS[0:64, b:b + 1024])
        P.c("scalar", cSC, ropes, waits=[W(cPE, None)])
        sc_rope.append(cSC.n)

        def ropev(e, b=b):
            e.tensor_tensor(rope1[:, :], PS[:, b:b + 1024], cosb[:, :], ALU.mult)
            return e.tensor_tensor(ropet[:, :], ropet[:, :], sinb[:, :], ALU.mult)
        P.c("vector", cVE, ropev, waits=[W(cSC, None)])
        P.c("vector", cVE, lambda e, q=qi: e.tensor_tensor(
            qctx[:, q * 1024:(q + 1) * 1024], rope1[:, :], ropet[:, :], ALU.add))
        ve_rope.append(cVE.n)
    q_done_pe = cPE.n
    q_done_ve = cVE.n

    dbg_dump("dbg_q", lambda: qctx[:, 0:4096], [W(cVE, q_done_ve)])

    P.phase = 'p1c'
    prep_rK, prep_rV, kt_read, v_read = [], [], [], []
    comp_done = []
    ve_ctx_done = []

    kvrows = kv_h.ap().rearrange("a b c d e -> (a b c) (d e)")

    def plan_prep(j):
        slq = j % 2
        dc = dS[slq]
        for g in range(4):
            w = [W(cVE, idx_done)]
            if j > 0:
                w.append(W(dS[(j - 1) % 2], prep_rV[j - 1]))
            P.dma("gpsimd", dKV, lambda e, g=g, k=j: e.indirect_dma_start(
                graw[:, g * 2048:(g + 1) * 2048], None, kvrows,
                IndirectOffsetOnAxis(sidxK[:, k * 4 + g:k * 4 + g + 1], 0)), w)
        w = [W(dKV, None)]
        if j >= 2:
            w.append(W(dc, kt_read[j - 2]))
        P.dma("sync", dc, lambda e, s=slq: e.dma_start(
            out=kvlK[s].ap().rearrange("(g p j) d -> p g j d", g=4, p=128, j=16),
            in_=graw[:, :].rearrange("p (g j d) -> p g j d", g=4, j=16, d=128)), w)
        prep_rK.append(dc.n)
        for g in range(4):
            P.dma("gpsimd", dKV, lambda e, g=g, k=j: e.indirect_dma_start(
                graw[:, g * 2048:(g + 1) * 2048], None, kvrows,
                IndirectOffsetOnAxis(sidxV[:, k * 4 + g:k * 4 + g + 1], 0)),
                [W(dc, prep_rK[j])])
        w = [W(dKV, None)]
        if j >= 2:
            w.append(W(dc, v_read[j - 2]))
        P.dma("sync", dc, lambda e, s=slq: e.dma_start(
            out=kvlV[s].ap().rearrange("(g p j) d -> p g j d", g=4, p=128, j=16),
            in_=graw[:, :].rearrange("p (g j d) -> p g j d", g=4, j=16, d=128)), w)
        prep_rV.append(dc.n)
        w = [W(dc, prep_rK[j]), W(cPE, q_done_pe)]
        if j >= 2:
            w.append(W(cPE, comp_done[j - 2]))
        P.dma("sync", dc, lambda e, s=slq: e.dma_start(
            out=KT[s][:, :], in_=kvlK[s].ap(), transpose=True), w)
        kt_read.append(dc.n)
        w = [W(dc, prep_rV[j]), W(cPE, q_done_pe)]
        if j >= 2:
            w.append(W(cPE, comp_done[j - 2]))
        P.dma("sync", dc, lambda e, s=slq: e.dma_start(
            out=Vb[s][:, :].rearrange("p (tt d) -> p tt d", d=128),
            in_=kvlV[s].ap().rearrange("(tt p) d -> p tt d", p=128)), w)
        v_read.append(dc.n)

    plan_prep(0)
    for j in range(8):
        if j + 1 < 8:
            plan_prep(j + 1)
        slq = j % 2
        for qh in range(4):
            h = j * 4 + qh
            hp = h % 2
            pe_scores, pe_ctx, sums, sc_exp = [], [], [], []

            def emit_scores(tt):
                par = tt % 2
                sbk = par * 1024

                def scmm(e, s=slq, t=tt, hh=h, b=sbk):
                    r = []
                    for half in range(2):
                        r.append(e.matmul(
                            ps(b + half * 512, b + half * 512 + 512),
                            KT[s][:, t * 128:(t + 1) * 128],
                            qctx[:, hh * 1024 + half * 512: hh * 1024 + half * 512 + 512],
                            start=True, stop=True))
                    return r
                w = [W(dS[slq], kt_read[j]), W(dS[slq], v_read[j]), W(cVE, q_done_ve)]
                if tt >= 2:
                    w.append(W(cSC, sc_exp[tt - 2]))
                elif h > 0:
                    w.append(W(cSC, prev_exp[62 + tt]))
                P.c("tensor", cPE, scmm, waits=w)
                pe_scores.append(cPE.n)

                w = [W(cPE, pe_scores[tt])]
                if tt >= 2:
                    w += [W(cPE, pe_ctx[tt - 2]), W(sums[tt - 2][0], sums[tt - 2][1])]
                elif h > 0:
                    w += [W(cPE, prev_ctx[62 + tt]),
                          W(prev_sums[62 + tt][0], prev_sums[62 + tt][1])]
                P.c("scalar", cSC, lambda e, p=par, b=sbk: e.activation(
                    ptile[p][:, :], ps(b, b + 1024), AF.Exp), waits=w)
                sc_exp.append(cSC.n)

            def emit_ctx(tt):
                par = tt % 2

                def ctmm(e, s=slq, t=tt, p=par):
                    r = []
                    for half in range(2):
                        r.append(e.matmul(
                            ps(2048 + half * 512, 2048 + half * 512 + 512),
                            Vb[s][:, t * 128:(t + 1) * 128],
                            ptile[p][:, half * 512:half * 512 + 512],
                            start=(t == 0), stop=(t == 63)))
                    return r
                w = [W(cSC, sc_exp[tt])]
                if tt == 0 and h > 0:
                    w.append(W(cVE, None))
                P.c("tensor", cPE, ctmm, waits=w)
                pe_ctx.append(cPE.n)

                pool = (tt % 8) < 3
                eng = "gpsimd" if pool else "vector"
                ctr = cGP if pool else cVE
                acc = (sumO if pool else sumV)[hp]
                first = (tt < 8) and ((tt % 8) == (0 if pool else 3))

                def sfn(e, p=par, a=acc, f=first):
                    if f:
                        return e.tensor_copy(a[:, :], ptile[p][:, :])
                    return e.tensor_tensor(a[:, :], a[:, :], ptile[p][:, :], ALU.add)
                sw = [W(cSC, sc_exp[tt])]
                if tt < 8 and h >= 2:
                    sw.append(W(cVE, None))
                P.c(eng, ctr, sfn, waits=sw)
                sums.append((ctr, ctr.n))

            def emit_epilogue():
                hq = h - 1
                hqp = hq % 2
                P.c("vector", cVE, lambda e, a=sumV[hqp], b=sumO[hqp]: e.tensor_tensor(
                    a[:, :], a[:, :], b[:, :], ALU.add),
                    waits=[W(prev_sums[63][0], prev_sums[63][1]),
                           W(prev_sums[62][0], prev_sums[62][1]),
                           W(cGP, None)])
                def tr8(e, a=sumV[hqp]):
                    r = []
                    for k in range(8):
                        r.append(e.transpose(PS[:, 3072 + k * 128:3200 + k * 128],
                                             a[:, k * 128:(k + 1) * 128], eye32[:, :]))
                    return r
                P.c("tensor", cPE, tr8, waits=[W(cVE, None)])
                P.c("vector", cVE, lambda e: e.reduce_sum(
                    ssums[:, 0:8], PS[:, 3072:4096].rearrange("p (k f) -> p k f", k=8), AX.X),
                    waits=[W(cPE, None)])
                P.c("vector", cVE, lambda e: e.reciprocal(srec[:, 0:8], ssums[:, 0:8]))

                def tr1(e):
                    r = []
                    for k in range(8):
                        r.append(e.transpose(PS[0:1, 3072 + k * 128:3072 + k * 128 + 128],
                                             srec[:, k:k + 1], eye32[:, :]))
                    return r
                P.c("tensor", cPE, tr1, waits=[W(cVE, None)])
                P.c("vector", cVE, lambda e: e.tensor_copy(srecro[0:1, :], PS[0:1, 3072:4096]),
                    waits=[W(cPE, None)])

                def bcmm(e):
                    r = []
                    for half in range(2):
                        r.append(e.matmul(ps(3072 + half * 512, 3072 + half * 512 + 512),
                                          ones32[:, :],
                                          srecro[0:1, half * 512:half * 512 + 512],
                                          start=True, stop=True))
                    return r
                P.c("tensor", cPE, bcmm, waits=[W(cVE, None)])
                P.c("vector", cVE, lambda e: e.tensor_copy(sbcast[:, :], ps(3072, 4096)),
                    waits=[W(cPE, None)])
                P.c("vector", cVE, lambda e, hh=hq: e.tensor_tensor(
                    qctx[:, hh * 1024:(hh + 1) * 1024], ps(2048, 3072), sbcast[:, :], ALU.mult),
                    waits=[W(cPE, prev_ctx[63])])
                ve_ctx_done.append(cVE.n)

            emit_scores(0)
            emit_scores(1)
            if h > 0:
                emit_epilogue()
            emit_ctx(0)
            for tt in range(2, 64):
                emit_scores(tt)
                emit_ctx(tt - 1)
            emit_ctx(63)
            prev_exp = sc_exp
            prev_ctx = pe_ctx
            prev_sums = sums
        comp_done.append(cPE.n)
    hqp = 31 % 2
    P.c("vector", cVE, lambda e, a=sumV[hqp], b=sumO[hqp]: e.tensor_tensor(
        a[:, :], a[:, :], b[:, :], ALU.add),
        waits=[W(prev_sums[63][0], prev_sums[63][1]),
               W(prev_sums[62][0], prev_sums[62][1]), W(cGP, None)])

    def tr8f(e, a=sumV[hqp]):
        r = []
        for k in range(8):
            r.append(e.transpose(PS[:, 3072 + k * 128:3200 + k * 128],
                                 a[:, k * 128:(k + 1) * 128], eye32[:, :]))
        return r
    P.c("tensor", cPE, tr8f, waits=[W(cVE, None)])
    P.c("vector", cVE, lambda e: e.reduce_sum(
        ssums[:, 0:8], PS[:, 3072:4096].rearrange("p (k f) -> p k f", k=8), AX.X),
        waits=[W(cPE, None)])
    P.c("vector", cVE, lambda e: e.reciprocal(srec[:, 0:8], ssums[:, 0:8]))

    def tr1f(e):
        r = []
        for k in range(8):
            r.append(e.transpose(PS[0:1, 3072 + k * 128:3072 + k * 128 + 128],
                                 srec[:, k:k + 1], eye32[:, :]))
        return r
    P.c("tensor", cPE, tr1f, waits=[W(cVE, None)])
    P.c("vector", cVE, lambda e: e.tensor_copy(srecro[0:1, :], PS[0:1, 3072:4096]),
        waits=[W(cPE, None)])

    def bcmmf(e):
        r = []
        for half in range(2):
            r.append(e.matmul(ps(3072 + half * 512, 3072 + half * 512 + 512),
                              ones32[:, :], srecro[0:1, half * 512:half * 512 + 512],
                              start=True, stop=True))
        return r
    P.c("tensor", cPE, bcmmf, waits=[W(cVE, None)])
    P.c("vector", cVE, lambda e: e.tensor_copy(sbcast[:, :], ps(3072, 4096)),
        waits=[W(cPE, None)])
    P.c("vector", cVE, lambda e: e.tensor_tensor(
        qctx[:, 31 * 1024:32 * 1024], ps(2048, 3072), sbcast[:, :], ALU.mult),
        waits=[W(cPE, prev_ctx[63])])
    ve_ctx_done.append(cVE.n)
    ctx_all_done = cVE.n

    dbg_dump("dbg_ctx", lambda: qctx[:, 0:4096], [W(cVE, ctx_all_done)])

    P.phase = 'p1d'
    hs_written = []
    x2_written = []
    last_wo_pe = q_done_pe
    for sh in range(2):
        pe_wo, ve_piece = [], []
        for oi in range(32):
            slot = oi % 2
            P.dma("gpsimd", dW[slot], lambda e, s=slot, o=oi: e.dma_start(
                out=wpan[s][:, :], in_=wo_h.ap()[o, :, :]), [W(cPE, slot_pe[slot])])

            def womm(e, s=slot, b=slot * 512, half=sh):
                r = []
                for fi in range(32):
                    r.append(e.matmul(
                        ps(b, b + 512), wpan[s][:, fi * 128:(fi + 1) * 128],
                        qctx[:, fi * 1024 + half * 512: fi * 1024 + half * 512 + 512],
                        start=(fi == 0), stop=(fi == 31)))
                return r
            w = [W(dW[slot], None), W(cVE, ctx_all_done)]
            if oi >= 2:
                w.append(W(cVE, ve_piece[oi - 2]))
            P.c("tensor", cPE, womm, waits=w)
            pe_wo.append(cPE.n)
            slot_pe[slot] = cPE.n
            P.c("vector", cVE, lambda e, b=slot * 512, s=slot: e.tensor_copy(
                swoc[s][:, :], ps(b, b + 512)), waits=[W(cPE, None)])
            evd = cVE.n

            def wotr(e, s=slot):
                r = []
                for k in range(4):
                    r.append(e.transpose(PS[:, 3072 + k * 128:3072 + k * 128 + 128],
                                         swoc[s][:, k * 128:(k + 1) * 128], eye32[:, :]))
                return r
            w = [W(cVE, evd)]
            if ve_piece:
                w.append(W(cVE, ve_piece[-1]))
            P.c("tensor", cPE, wotr, waits=w)
            pw = [W(cPE, None)]
            if sh == 1 and oi == 0:
                pw.append(W(dHW, hs_written[3]))
                pw.append(W(dX2W, x2_written[3]))
            P.c("vector", cVE, lambda e, o=oi: e.tensor_copy(
                attnTMall[:, :].rearrange("p (k f) -> p k f", k=4)[:, :, o * 128:(o + 1) * 128],
                PS[:, 3072:3584].rearrange("p (k f) -> p k f", k=4)), waits=pw)
            ve_piece.append(cVE.n)
        wo_half_done = cVE.n
        last_wo_pe = cPE.n

        for sj in range(4):
            st = sh * 4 + sj
            ln_stats((lambda k=sj: attnTM[k][:, :]), waits=[W(cVE, wo_half_done)])
            P.c("vector", cVE, lambda e, k=sj: e.tensor_scalar(
                attnTM[k][:, :], attnTM[k][:, :], snm[:, 0:1], srs[:, 0:1], ALU.add, ALU.mult),
                waits=[W(cVE, None)])
            P.dma("sync", dXT, lambda e, s=st: e.dma_start(
                out=xtile[:, :], in_=x_h.ap()[s * 128:(s + 1) * 128, :]), [W(cVE, None)])
            P.c("vector", cVE, lambda e, k=sj: e.tensor_tensor(
                attnTM[k][:, :], attnTM[k][:, :], xtile[:, :], ALU.add), waits=[W(dXT, None)])
            x2done = cVE.n
            P.dma("sync", dX2W, lambda e, s=st, k=sj: e.dma_start(
                out=x2scr.ap()[s * 128:(s + 1) * 128, :], in_=attnTM[k][:, :]),
                [W(cVE, x2done)])
            x2_written.append(dX2W.n)
            ln_stats((lambda k=sj: attnTM[k][:, :]), waits=[W(cVE, x2done)])
            P.c("vector", cVE, lambda e, k=sj: e.tensor_scalar(
                htile[:, :], attnTM[k][:, :], snm[:, 0:1], srs[:, 0:1], ALU.add, ALU.mult),
                waits=[W(cVE, None)])
            P.c("vector", cVE, lambda e, s=st: e.reduce_sum(
                shs[:, s:s + 1], htile[:, :], AX.X, apply_absolute_value=True),
                waits=[W(cVE, None)])
            w = [W(cVE, None)]
            if sh == 0 and sj == 0:
                w.append(W(dS[1], prep_rV[7]))
            else:
                w.append(W(dHW, hs_written[-1]))
            P.c("vector", cVE, lambda e: e.tensor_copy(h16t[:, :], htile[:, :]), waits=w)
            P.dma("sync", dHW, lambda e, s=st: e.dma_start(
                out=hscr.ap()[s * 128:(s + 1) * 128, :], in_=h16t[:, :]), [W(cVE, None)])
            hs_written.append(dHW.n)
    p1d_ve = cVE.n
    dbg_dump("dbg_h", lambda: hscr.ap()[0:128, :], [W(dHW, hs_written[0])])

    dbg_dump("dbg_x2", lambda: attnTM[3][:, :], [W(cVE, p1d_ve)])

    P.phase = 'p2'
    for fc in range(32):
        P.dma("sync", dT, lambda e, f=fc: e.dma_start(
            out=shT[:, f * 512:(f + 1) * 512],
            in_=hscr.ap()[0:512, f * 128:(f + 1) * 128], transpose=True),
            [W(dHW, hs_written[3]), W(cVE, p1d_ve)])
    shT0_done = dT.n

    def gatemm(e):
        r = []
        for fi in range(32):
            r.append(e.matmul(PS[0:3, 3072:3584], sgw[:, fi * 3:(fi + 1) * 3],
                              shT[:, fi * 512:(fi + 1) * 512],
                              start=(fi == 0), stop=(fi == 31)))
        return r
    P.c("tensor", cPE, gatemm, waits=[W(dT, shT0_done), W(dGM, p0g), W(cVE, p1d_ve)])
    P.c("vector", cVE, lambda e: e.tensor_copy(sglog[0:3, :], PS[0:3, 3072:3584]),
        waits=[W(cPE, None)])
    P.c("tensor", cPE, lambda e: e.transpose(PS[0:1, 3584:3587], sglog[0:3, 0:1],
                                             eye32[0:3, 0:3]), waits=[W(cVE, None)])
    P.c("vector", cVE, lambda e: e.tensor_copy(srow1[0:1, 0:3], PS[0:1, 3584:3587]),
        waits=[W(cPE, None)])

    P.c("vector", cVE, lambda e: e.tensor_tensor(stt1[:, :], shs[:, :], smv[:, :], ALU.mult),
        waits=[W(cVE, p1d_ve)])
    P.c("vector", cVE, lambda e: e.reduce_sum(sgs1[:, :], stt1[:, :], AX.X))
    P.c("vector", cVE, lambda e: e.tensor_tensor(stt1[:, :], shs[:, :], smt[:, :], ALU.mult))
    P.c("vector", cVE, lambda e: e.reduce_sum(sgs2[:, :], stt1[:, :], AX.X))
    P.c("tensor", cPE, lambda e: e.transpose(PS[0:1, 3600:3728], sgs1[:, 0:1], eye32[:, :]),
        waits=[W(cVE, None)])
    P.c("vector", cVE, lambda e: e.reduce_sum(spay[0:1, 0:1], PS[0:1, 3600:3728], AX.X),
        waits=[W(cPE, None)])
    P.c("tensor", cPE, lambda e: e.transpose(PS[0:1, 3600:3728], sgs2[:, 0:1], eye32[:, :]),
        waits=[W(cVE, None)])
    P.c("vector", cVE, lambda e: e.reduce_sum(spay[0:1, 1:2], PS[0:1, 3600:3728], AX.X),
        waits=[W(cPE, None)])
    P.c("vector", cVE, lambda e: e.tensor_copy(spay[0:1, 2:5], srow1[0:1, 0:3]))
    pay_done = cVE.n

    def exchange(e):
        par_r = e.alloc_register("par_r")
        e.reg_load(par_r, spar[0:1, 0:1])
        pnot_r = e.alloc_register("pnot_r")
        e.reg_alu(pnot_r, par_r, 1, ALU.bitwise_xor)
        parv = e.snap(par_r, min_val=0, max_val=1)
        pnotv = e.snap(pnot_r, min_val=0, max_val=1)
        e.dma_start(out=shd_h.ap()[bass.ts(parv, 1), :], in_=spay[0:1, :]).then_inc(swx.sem, 16)
        e.wait_ge(swx.sem, 16)
        e.dma_start(out=sht_h.ap()[bass.ts(parv, 1), :], in_=stag[0:1, :],
                    allow_slow_non_contiguous=True).then_inc(swx.sem, 16)
        texp_r = e.alloc_register("texp_r")
        e.reg_load(texp_r, stag[0:1, 0:1])
        tcur_r = e.alloc_register("tcur_r")
        nm_r = e.alloc_register("nm_r")
        e.reg_mov(nm_r, 1)
        with e.While(nm_r):
            e.dma_start(out=stagp[:, :], in_=sht_h.ap()[bass.ts(pnotv, 1), 0:1],
                        allow_slow_non_contiguous=True).then_inc(sdump.sem, 16)
            e.reg_load(tcur_r, stagp[0:1, 0:1])
            e.reg_sub(nm_r, tcur_r, texp_r)
        r = e.dma_start(out=spart[0:1, :], in_=shd_h.ap()[bass.ts(pnotv, 1), :])
        return r
    P.c("sync", None, lambda e: exchange(e).then_inc(spx.sem, 16) and e.nop(),
        waits=[W(cVE, pay_done)])

    EXW = [W(spx, 16), W(cVE, pay_done)]
    P.c("vector", cVE, lambda e: e.tensor_tensor(
        srow2[0:1, 0:2], spay[0:1, 0:2], spart[0:1, 0:2], ALU.add), waits=EXW)
    P.c("vector", cVE, lambda e: e.tensor_tensor(
        srow1[0:1, 0:3], spart[0:1, 2:5], spay[0:1, 2:5], ALU.subtract))
    P.c("vector", cVE, lambda e: e.tensor_scalar(
        srow1[0:1, 0:3], srow1[0:1, 0:3], sparf[0:1, 0:1], None, ALU.mult))
    P.c("vector", cVE, lambda e: e.tensor_tensor(
        sg0[0:1, 0:3], spay[0:1, 2:5], srow1[0:1, 0:3], ALU.add))
    P.c("vector", cVE, lambda e: e.tensor_scalar(
        sgs3[0:1, 0:1], srow2[0:1, 1:2], 1.0 / ((SEQ - VIS) * D), None, ALU.mult))
    P.c("vector", cVE, lambda e: e.tensor_scalar(
        srow1[0:1, 0:1], srow2[0:1, 0:1], 1.0 / (VIS * D), 1e-8, ALU.mult, ALU.add))
    P.c("vector", cVE, lambda e: e.reciprocal(srow2[0:1, 2:3], srow1[0:1, 0:1]))
    P.c("vector", cVE, lambda e: e.tensor_tensor(
        sgs3[0:1, 0:1], sgs3[0:1, 0:1], srow2[0:1, 2:3], ALU.mult))
    P.c("vector", cVE, lambda e: e.tensor_scalar(
        scv[0:1, 0:3], sgwl[0:1, 0:3], sgs3[0:1, 0:1], None, ALU.mult))
    P.c("vector", cVE, lambda e: e.tensor_tensor(
        scv[0:1, 0:3], scv[0:1, 0:3], sgbr[0:1, 0:3], ALU.add))
    P.c("vector", cVE, lambda e: e.tensor_tensor(
        sg0[0:1, 0:3], sg0[0:1, 0:3], scv[0:1, 0:3], ALU.add))
    P.c("vector", cVE, lambda e: e.reduce_max(srow1[0:1, 0:1], sg0[0:1, 0:3], AX.X))
    P.c("vector", cVE, lambda e: e.tensor_reduce(srow1[0:1, 1:2], sg0[0:1, 0:3], AX.X, ALU.min))
    P.c("vector", cVE, lambda e: e.tensor_scalar(
        sh0[0:1, 0:3], sg0[0:1, 0:3], srow1[0:1, 0:1], None, ALU.is_ge))
    P.c("vector", cVE, lambda e: e.tensor_scalar(
        sh1[0:1, 0:3], sg0[0:1, 0:3], srow1[0:1, 1:2], None, ALU.is_le))
    P.c("vector", cVE, lambda e: e.tensor_tensor(
        sh1[0:1, 0:3], sh1[0:1, 0:3], sh0[0:1, 0:3], ALU.add))
    P.c("vector", cVE, lambda e: e.tensor_scalar(
        sh1[0:1, 0:3], sh1[0:1, 0:3], -1.0, 1.0, ALU.mult, ALU.add))
    router_ve = cVE.n
    P.c("tensor", cPE, lambda e: e.matmul(PS[:, 3587:3590], ones32[:, :], scv[0:1, 0:3],
                                          start=True, stop=True), waits=[W(cVE, router_ve)])
    P.c("vector", cVE, lambda e: e.tensor_copy(scvb[:, 0:3], PS[:, 3587:3590]),
        waits=[W(cPE, None)])
    P.c("tensor", cPE, lambda e: e.matmul(PS[:, 3584:3587], ones32[:, :], sh0[0:1, 0:3],
                                          start=True, stop=True), waits=[W(cVE, router_ve)])
    P.c("vector", cVE, lambda e: e.tensor_copy(hbc[0][:, 0:3], PS[:, 3584:3587]),
        waits=[W(cPE, None)])
    P.c("tensor", cPE, lambda e: e.matmul(PS[:, 3584:3587], ones32[:, :], sh1[0:1, 0:3],
                                          start=True, stop=True), waits=[W(cVE, None)])
    P.c("vector", cVE, lambda e: e.tensor_copy(hbc[1][:, 0:3], PS[:, 3584:3587]),
        waits=[W(cPE, None)])
    for i in range(2):
        P.c("vector", cVE, lambda e, i=i: e.tensor_scalar(
            b2c[i][:, :], sb2t[:, 0:32], hbc[i][:, 0:1], None, ALU.mult))
        P.c("vector", cVE, lambda e, i=i: e.scalar_tensor_tensor(
            b2c[i][:, :], sb2t[:, 32:64], hbc[i][:, 1:2], b2c[i][:, :], ALU.mult, ALU.add))
        P.c("vector", cVE, lambda e, i=i: e.scalar_tensor_tensor(
            b2c[i][:, :], sb2t[:, 64:96], hbc[i][:, 2:3], b2c[i][:, :], ALU.mult, ALU.add))
    router_done = cVE.n

    dbg_dump("dbg_pay", lambda: spay[0:1, :], [W(cVE, router_done)])
    dbg_dump("dbg_g0", lambda: sg0[0:1, :], [W(cVE, router_done)])
    dbg_dump("dbg_h01", lambda: sh0[0:1, :], [W(cVE, router_done)])

    P.phase = 'p3'
    last_w1 = {}
    for sh in range(2):
        if sh == 1:
            for fc in range(32):
                P.dma("sync", dT, lambda e, f=fc: e.dma_start(
                    out=shT[:, f * 512:(f + 1) * 512],
                    in_=hscr.ap()[512:1024, f * 128:(f + 1) * 128], transpose=True),
                    [W(dHW, hs_written[7]), W(cVE, None), W(cPE, None)])
            sh1_shT = dT.n
            P.c("tensor", cPE, gatemm, waits=[W(dT, sh1_shT), W(cVE, None)])
            P.c("vector", cVE, lambda e: e.tensor_copy(sglog[0:3, :], PS[0:3, 3072:3584]),
                waits=[W(cPE, None)])

        for sj in range(4):
            P.c("tensor", cPE, lambda e, k=sj: e.transpose(
                PS[:, 3072:3075], sglog[0:3, k * 128:(k + 1) * 128], eye32[0:3, 0:3]),
                waits=[W(cVE, None)])
            P.c("vector", cVE, lambda e, k=sj: e.tensor_copy(sgt[k][:, 0:3], PS[:, 3072:3075]),
                waits=[W(cPE, None)])

            P.c("vector", cVE, lambda e, k=sj: e.tensor_tensor(
                sgt[k][:, 0:3], sgt[k][:, 0:3], scvb[:, 0:3], ALU.add),
                waits=[W(cVE, router_done)])
            P.c("vector", cVE, lambda e, k=sj: e.reduce_max(
                sgs1[:, :], sgt[k][:, 0:3], AX.X, negate=True))
            P.c("scalar", cSC, lambda e, k=sj: e.activation(
                sgx[:, 0:3], sgt[k][:, 0:3], AF.Exp, bias=sgs1[:, 0:1]), waits=[W(cVE, None)])

            for gfn in [
                lambda e: e.reduce_sum(sgs2[:, :], sgx[:, 0:3], AX.X),
                lambda e: e.reciprocal(sgs1[:, :], sgs2[:, :]),
                lambda e: e.tensor_scalar(sgx[:, 0:3], sgx[:, 0:3], sgs1[:, 0:1], None, ALU.mult),
                lambda e: e.reduce_max(sgs1[:, :], sgx[:, 0:3], AX.X),
                lambda e: e.tensor_reduce(sgs2[:, :], sgx[:, 0:3], AX.X, ALU.min),
                lambda e: e.reduce_sum(sgs3[:, :], sgx[:, 0:3], AX.X),
                lambda e: e.tensor_tensor(sgs3[:, :], sgs3[:, :], sgs1[:, :], ALU.subtract),
                lambda e: e.tensor_tensor(sgs3[:, :], sgs3[:, :], sgs2[:, :], ALU.subtract),
                lambda e: e.tensor_tensor(sa1[:, :], sgs1[:, :], sgs3[:, :], ALU.add),
                lambda e: e.tensor_scalar(sa1[:, :], sa1[:, :], 1.0, 1e-8, ALU.mult, ALU.add),
                lambda e: e.reciprocal(sa0[:, :], sa1[:, :]),
                lambda e: e.tensor_tensor(sa1[:, :], sgs3[:, :], sa0[:, :], ALU.mult),
                lambda e: e.tensor_tensor(sa0[:, :], sgs1[:, :], sa0[:, :], ALU.mult),
            ]:
                P.c("vector", cVE, gfn, waits=[W(cSC, None)])
            P.c("tensor", cPE, lambda e: e.transpose(PS[0:1, 3072:3200], sa0[:, 0:1],
                                                     eye32[:, :]), waits=[W(cVE, None)])
            P.c("vector", cVE, lambda e, k=sj: e.tensor_copy(
                a0row[0:1, k * 128:(k + 1) * 128], PS[0:1, 3072:3200]), waits=[W(cPE, None)])
            P.c("tensor", cPE, lambda e: e.transpose(PS[0:1, 3072:3200], sa1[:, 0:1],
                                                     eye32[:, :]), waits=[W(cVE, None)])
            P.c("vector", cVE, lambda e, k=sj: e.tensor_copy(
                a1row[0:1, k * 128:(k + 1) * 128], PS[0:1, 3072:3200]), waits=[W(cPE, None)])
        P.c("tensor", cPE, lambda e: e.matmul(ps(3584, 4096), ones32[:, :], a0row[0:1, :],
                                              start=True, stop=True), waits=[W(cVE, None)])
        P.c("vector", cVE, lambda e: e.tensor_copy(a0bc[:, :], ps(3584, 4096)),
            waits=[W(cPE, None)])
        P.c("tensor", cPE, lambda e: e.matmul(ps(3584, 4096), ones32[:, :], a1row[0:1, :],
                                              start=True, stop=True), waits=[W(cVE, None)])
        P.c("vector", cVE, lambda e: e.tensor_copy(a1bc[:, :], ps(3584, 4096)),
            waits=[W(cPE, None)])
        for ee in range(3):
            P.c("vector", cVE, lambda e, x=ee: e.tensor_scalar(
                sce[x][:, :], a0bc[:, :], hbc[0][:, x:x + 1], None, ALU.mult))
            P.c("vector", cVE, lambda e, x=ee: e.scalar_tensor_tensor(
                sce[x][:, :], a1bc[:, :], hbc[1][:, x:x + 1], sce[x][:, :], ALU.mult, ALU.add))
        gate_fin = cVE.n

        if sh == 0:
            dbg_dump("dbg_gate", lambda: sgt[0][:, :], [W(cVE, gate_fin)])
            dbg_dump("dbg_a0", lambda: a0row[0:1, :], [W(cVE, gate_fin)])
            dbg_dump("dbg_sce", lambda: sce[0][:, :], [W(cVE, gate_fin)])

        P.phase = 'w1_%d' % sh
        last_ve_act = 0
        for ee in range(3):
            fn = [AF.Tanh, AF.Gelu, AF.Silu][ee]
            for gi in range(32):
                slot = gi % 2
                P.dma("gpsimd", dW[slot], lambda e, s=slot, x=ee, g=gi: e.dma_start(
                    out=wpan[s][:, :], in_=w1_h.ap()[x, g, :, :]), [W(cPE, slot_pe[slot])])

                def w1mm(e, s=slot, b=slot * 512):
                    r = []
                    for fi in range(32):
                        r.append(e.matmul(ps(b, b + 512), wpan[s][:, fi * 128:(fi + 1) * 128],
                                          shT[:, fi * 512:(fi + 1) * 512],
                                          start=(fi == 0), stop=(fi == 31)))
                    return r
                w = [W(dW[slot], None), W(cVE, gate_fin)]
                key = (ee, gi - 2) if gi >= 2 else (ee - 1, 30 + gi)
                if key in last_w1:
                    w.append(W(cVE, last_w1[key]))
                P.c("tensor", cPE, w1mm, waits=w)
                slot_pe[slot] = cPE.n
                P.c("scalar", cSC, lambda e, b=slot * 512, x=ee, g=gi, f=fn: e.activation(
                    sactf[:, :], ps(b, b + 512), f, bias=sb1t[:, x * 32 + g:x * 32 + g + 1]),
                    waits=[W(cPE, None), W(cVE, last_ve_act)])
                va_w = [W(cSC, None)]
                if ee == 2 and gi == 0:
                    va_w.append(W(dHW, hs_written[7]))
                    va_w.append(W(dX2W, x2_written[7]))
                P.c("vector", cVE, lambda e, x=ee, g=gi: e.tensor_tensor(
                    act[x][:, g * 512:(g + 1) * 512], sactf[:, :], sce[x][:, :], ALU.mult),
                    waits=va_w)
                last_ve_act = cVE.n
                last_w1[(ee, gi)] = cVE.n
        act_done = cVE.n

        P.phase = 'w2_%d' % sh
        ve_moe = []
        moe_w = []
        for oi in range(32):
            for ee in range(3):
                slot = (oi * 3 + ee) % 2
                P.dma("gpsimd", dW[slot], lambda e, s=slot, x=ee, o=oi: e.dma_start(
                    out=wpan[s][:, :], in_=w2_h.ap()[x, o, :, :]), [W(cPE, slot_pe[slot])])

                def w2mm(e, s=slot, x=ee, b=2048 + (oi % 2) * 512):
                    r = []
                    for gi in range(32):
                        r.append(e.matmul(ps(b, b + 512), wpan[s][:, gi * 128:(gi + 1) * 128],
                                          act[x][:, gi * 512:(gi + 1) * 512],
                                          start=(x == 0 and gi == 0), stop=(x == 2 and gi == 31)))
                    return r
                w = [W(dW[slot], None), W(cVE, act_done)]
                if oi >= 2 and ee == 0:
                    w.append(W(cVE, ve_moe[oi - 2]))
                P.c("tensor", cPE, w2mm, waits=w)
                slot_pe[slot] = cPE.n

            w = [W(cPE, None)]
            if moe_w:
                w.append(W(dMW, moe_w[-1]))
            P.c("vector", cVE, lambda e, o=oi, b=2048 + (oi % 2) * 512: e.scalar_tensor_tensor(
                smoec[:, :], a0bc[:, :], b2c[0][:, o:o + 1],
                ps(b, b + 512), ALU.mult, ALU.add), waits=w)
            P.c("vector", cVE, lambda e, o=oi: e.scalar_tensor_tensor(
                smoec[:, :], a1bc[:, :], b2c[1][:, o:o + 1], smoec[:, :], ALU.mult, ALU.add))
            ve_moe.append(cVE.n)
            P.dma("sync", dMW, lambda e, o=oi, half=sh: e.dma_start(
                out=moescr.ap()[o * 128:(o + 1) * 128, half * 512:half * 512 + 512],
                in_=smoec[:, :]), [W(cVE, None)])
            moe_w.append(dMW.n)
        moe_written = dMW.n

        P.phase = 'p4_%d' % sh
        ve_fin = []
        for oi in range(32):
            w = [W(dMW, moe_written)]
            if ve_fin:
                w.append(W(cVE, ve_fin[-1]))
            P.dma("sync", dMS, lambda e, o=oi, half=sh: e.dma_start(
                out=mstg[:, :],
                in_=moescr.ap()[o * 128:(o + 1) * 128, half * 512:half * 512 + 512]), w)
            def p4tr(e):
                r = []
                for k in range(4):
                    r.append(e.transpose(PS[:, 3072 + k * 128:3072 + k * 128 + 128],
                                         mstg[:, k * 128:(k + 1) * 128], eye32[:, :]))
                return r
            w = [W(dMS, None)]
            if ve_fin:
                w.append(W(cVE, ve_fin[-1]))
            P.c("tensor", cPE, p4tr, waits=w)
            P.c("vector", cVE, lambda e, o=oi: e.tensor_copy(
                smtmall[:, :].rearrange("p (k f) -> p k f", k=4)[:, :, o * 128:(o + 1) * 128],
                PS[:, 3072:3584].rearrange("p (k f) -> p k f", k=4)), waits=[W(cPE, None)])
            ve_fin.append(cVE.n)
        fin_pieces = cVE.n
        for sj in range(4):
            st = sh * 4 + sj
            ln_stats((lambda k=sj: smtm[k][:, :]), waits=[W(cVE, fin_pieces)])
            P.c("vector", cVE, lambda e, k=sj: e.tensor_scalar(
                smtm[k][:, :], smtm[k][:, :], snm[:, 0:1], srs[:, 0:1], ALU.add, ALU.mult),
                waits=[W(cVE, None)])
            w = [W(dX2W, x2_written[7]), W(cVE, None)]
            if dOUT.n:
                w.append(W(dOUT, dOUT.n))
            P.dma("sync", dX2, lambda e, s=st: e.dma_start(
                out=x2t[:, :], in_=x2scr.ap()[s * 128:(s + 1) * 128, :]), w)
            P.c("vector", cVE, lambda e, k=sj: e.tensor_tensor(
                x2t[:, :], smtm[k][:, :], x2t[:, :], ALU.add), waits=[W(dX2, None)])
            P.dma("sync", dOUT, lambda e, s=st: e.dma_start(
                out=out_h.ap()[s * 128:(s + 1) * 128, :], in_=x2t[:, :]), [W(cVE, None)])

    P.raw("sync", lambda e: e.wait_ge(dOUT.sem, dOUT.n))

    with nc.Block() as block:
        P.replay(block)
    return nc


def kernel(L0_attn_hidden_states, L0_attn_Wq, L0_attn_Wo, L0_attn_kv_cache,
           L0_attn_block_table, L0_ffn_gate_w, L0_ffn_gate_b, L0_ffn_w1,
           L0_ffn_b1, L0_ffn_w2, L0_ffn_b2, _dbg=(), _nc=None, _raw=False):
    x = np.ascontiguousarray(np.asarray(L0_attn_hidden_states, dtype=np.float32).reshape(8192, 4096))
    Wq = np.asarray(L0_attn_Wq, np.float32)
    Wo = np.asarray(L0_attn_Wo, np.float32)
    w1 = np.asarray(L0_ffn_w1, np.float32)
    w2 = np.asarray(L0_ffn_w2, np.float32)
    kv = np.ascontiguousarray(np.asarray(L0_attn_kv_cache, np.float32))
    bt = np.ascontiguousarray(np.asarray(L0_attn_block_table, np.int32))
    gwf = np.asarray(L0_ffn_gate_w, np.float32)
    gbf = np.asarray(L0_ffn_gate_b, np.float32)
    b1 = np.asarray(L0_ffn_b1, np.float32)
    b2 = np.asarray(L0_ffn_b2, np.float32)

    wq_t, wo_t, w1_t, w2_t = _prep_weights(Wq, Wo, w1, w2)
    gw_t = np.ascontiguousarray(gwf[:4096].reshape(32, 128, 3))
    gwl = np.zeros((1, 4), np.float32)
    gwl[0, :3] = gwf[4096]
    gb = np.zeros((1, 4), np.float32)
    gb[0, :3] = gbf
    b1t = np.ascontiguousarray(b1.reshape(3, 32, 128).transpose(2, 0, 1).reshape(128, 96))
    b2t = np.ascontiguousarray(b2.reshape(3, 32, 128).transpose(2, 0, 1).reshape(128, 96))

    tag = np.array([[np.random.randint(1, 2 ** 30)]], np.int32)
    in_maps = []
    for c in range(NCORES):
        half = c % 2
        cos, sin = _rope_tables(half)
        mv, mt = _masks(half)
        in_maps.append({
            "x": x[c * 1024:(c + 1) * 1024],
            "wq": wq_t, "wo": wo_t, "w1": w1_t, "w2": w2_t,
            "kv": kv, "bt": bt, "gw": gw_t, "gwl": gwl, "gb": gb,
            "b1t": b1t, "b2t": b2t,
            "cost": cos, "sint": sin, "maskv": mv, "maskt": mt,
            "par": np.array([[half]], np.int32),
            "parf": np.array([[float(half)]], np.float32),
            "tag": tag,
        })

    nc = _nc if _nc is not None else build_program(dbg=_dbg)
    res = run_bass_kernel_spmd(nc, in_maps, core_ids=list(range(NCORES)))
    out = np.concatenate([res.results[c]["out"] for c in range(NCORES)], axis=0)
    if _raw:
        return out.reshape(4, 2048, 4096), res
    return out.reshape(4, 2048, 4096)


# version 04
# speedup vs baseline: 1.1232x; 1.1232x over previous
import numpy as np
import ml_dtypes
import concourse.bass as bass
import concourse.mybir as mybir
from concourse.bass import IndirectOffsetOnAxis
from concourse.bass_utils import run_bass_kernel_spmd

F32 = mybir.dt.float32
BF16 = mybir.dt.bfloat16
I32 = mybir.dt.int32
AF = mybir.ActivationFunctionType
ALU = mybir.AluOpType
AX = mybir.AxisListType

NCORES = 8
S = 1024
D = 4096
EPS = 1e-5
VIS = 614
SEQ = 2048

SB_BASE = 16512
OFF_A = 0
OFF_B = 65536
OFF_WPAN = 131072
OFF_COS = 147456
OFF_GRAW = 155648
OFF_E = 172032
OFF_F = 204800
SB_LIMIT = 212863


class Ctr:
    __slots__ = ("sem", "n", "name")

    def __init__(self, nc, name):
        self.sem = nc.alloc_semaphore(name)
        self.n = 0
        self.name = name


class Plan:
    def __init__(self):
        self.ops = []
        self.self_ctr = {}
        self.phase = "init"

    def dma(self, eng, ctr, fn, waits=()):
        w = [(c, c.n if v is None else v) for (c, v) in waits]
        self.ops.append((eng, fn, w, [(ctr, 16)], self.phase))
        ctr.n += 16

    def c(self, eng, ctr, fn, waits=(), inc=1):
        w = [(c, c.n if v is None else v) for (c, v) in waits]
        sc = self.self_ctr.get(eng)
        if sc is not None and sc.n > 0:
            w.append((sc, sc.n))
        incs = [(ctr, inc)] if ctr is not None else []
        self.ops.append((eng, fn, w, incs, self.phase))
        if ctr is not None:
            ctr.n += inc

    def raw(self, eng, fn):
        self.ops.append((eng, fn, [], [], self.phase))

    def replay(self, block):
        engines = {"sync": block.sync, "gpsimd": block.gpsimd,
                   "scalar": block.scalar, "vector": block.vector,
                   "tensor": block.tensor}

        nc = self.nc

        def emit_for(name):
            def go(eng):
                hwm = {}
                cur = [None]
                for (en, fn, waits, incs, tag) in self.ops:
                    if en != name:
                        continue
                    if nc is not None and tag != cur[0]:
                        if cur[0] is not None:
                            nc.leave_named_scope(cur[0][0], cur[0][1], False)
                        sid = nc.enter_named_scope(tag, False)[0]
                        cur[0] = (tag, sid)
                    for (c, v) in waits:
                        if v > 0 and hwm.get(id(c), -1) < v:
                            eng.wait_ge(c.sem, v)
                            hwm[id(c)] = v
                    r = fn(eng)
                    if incs:
                        insts = r if isinstance(r, list) else [r]
                        for (c, k) in incs:
                            insts[-1].then_inc(c.sem, k)
                if nc is not None and cur[0] is not None:
                    nc.leave_named_scope(cur[0][0], cur[0][1], False)
            return go

        for name, deco in engines.items():
            deco(emit_for(name))


def W(ctr, v=None):
    return (ctr, v)


def _prep_weights(Wq, Wo, w1, w2):
    def tile2(w):
        return np.ascontiguousarray(
            w.reshape(32, 128, 32, 128).transpose(2, 1, 0, 3).reshape(32, 128, 4096))

    def tile3(w):
        return np.ascontiguousarray(
            w.reshape(3, 32, 128, 32, 128).transpose(0, 3, 2, 1, 4).reshape(3, 32, 128, 4096))

    return tile2(Wq), tile2(Wo), tile3(w1), tile3(w2)


def _rope_tables(half):
    inv = 1.0 / (10000.0 ** (np.arange(0, 64, dtype=np.float64) / 64.0))
    d = np.arange(128)
    pos = half * 1024 + np.arange(1024, dtype=np.float64)
    ang = pos[None, :] * inv[d % 64][:, None]
    sc = 1.0 / np.sqrt(128.0)
    return (np.cos(ang) * sc).astype(np.float32), (np.sin(ang) * sc).astype(np.float32)


def _masks(half):
    g = half * 1024 + (np.arange(8)[None, :] * 128 + np.arange(128)[:, None])
    return (g < VIS).astype(np.float32), (g >= VIS).astype(np.float32)


def build_program(dbg=(), stop_after=None):
    nc = bass.Bass("TRN2", target_bir_lowering=False, debug=False)

    x_h = nc.dram_tensor("x", [S, D], F32, kind="ExternalInput")
    wq_h = nc.dram_tensor("wq", [32, 128, 4096], F32, kind="ExternalInput")
    wo_h = nc.dram_tensor("wo", [32, 128, 4096], F32, kind="ExternalInput")
    w1_h = nc.dram_tensor("w1", [3, 32, 128, 4096], F32, kind="ExternalInput")
    w2_h = nc.dram_tensor("w2", [3, 32, 128, 4096], F32, kind="ExternalInput")
    kv_h = nc.dram_tensor("kv", [1024, 2, 8, 16, 128], F32, kind="ExternalInput")
    bt_h = nc.dram_tensor("bt", [512], I32, kind="ExternalInput")
    gw_h = nc.dram_tensor("gw", [32, 128, 3], F32, kind="ExternalInput")
    gwl_h = nc.dram_tensor("gwl", [1, 4], F32, kind="ExternalInput")
    gb_h = nc.dram_tensor("gb", [1, 4], F32, kind="ExternalInput")
    b1t_h = nc.dram_tensor("b1t", [128, 96], F32, kind="ExternalInput")
    b2t_h = nc.dram_tensor("b2t", [128, 96], F32, kind="ExternalInput")
    cos_h = nc.dram_tensor("cost", [128, 1024], F32, kind="ExternalInput")
    sin_h = nc.dram_tensor("sint", [128, 1024], F32, kind="ExternalInput")
    mv_h = nc.dram_tensor("maskv", [128, 8], F32, kind="ExternalInput")
    mt_h = nc.dram_tensor("maskt", [128, 8], F32, kind="ExternalInput")
    par_h = nc.dram_tensor("par", [1, 1], I32, kind="ExternalInput")
    parf_h = nc.dram_tensor("parf", [1, 1], F32, kind="ExternalInput")
    tag_h = nc.dram_tensor("tag", [1, 1], I32, kind="ExternalInput")
    out_h = nc.dram_tensor("out", [S, D], F32, kind="ExternalOutput")

    shd_h = nc.dram_tensor("shd", [2, 8], F32, kind="Internal", addr_space="Shared")
    sht_h = nc.dram_tensor("sht", [2, 1], I32, kind="Internal", addr_space="Shared")

    x1scr = nc.dram_tensor("x1scr", [S, D], BF16, kind="Internal")
    hscr = nc.dram_tensor("hscr", [S, D], BF16, kind="Internal")
    x2scr = nc.dram_tensor("x2scr", [S, D], F32, kind="Internal")
    moescr = nc.dram_tensor("moescr", [D, S], F32, kind="Internal")
    kvlK = [nc.dram_tensor(f"kvlK{i}", [8192, 128], BF16, kind="Internal") for i in range(2)]
    kvlV = [nc.dram_tensor(f"kvlV{i}", [8192, 128], BF16, kind="Internal") for i in range(2)]

    dbg_h = {name: nc.dram_tensor(name, shape, F32, kind="ExternalOutput")
             for (name, shape) in dbg}

    eye16_h = nc.inline_tensor(np.eye(128, dtype=ml_dtypes.bfloat16), "eye16c")
    eye32_h = nc.inline_tensor(np.eye(128, dtype=np.float32), "eye32c")
    ones16_h = nc.inline_tensor(np.ones((1, 128), dtype=ml_dtypes.bfloat16), "ones16c")
    ones32_h = nc.inline_tensor(np.ones((1, 128), dtype=np.float32), "ones32c")

    def sb(name, shape, dtype, off):
        return nc.alloc_sbuf_tensor_at(name, shape, dtype, offset=SB_BASE + off)

    qctx = sb("qctx", [128, 32 * 1024], BF16, OFF_A)
    x1T = sb("x1T", [128, 32 * 1024], BF16, OFF_B)
    KT = [sb(f"KT{i}", [128, 8192], BF16, OFF_B + i * 32768) for i in range(2)]
    Vb = [sb(f"Vb{i}", [128, 8192], BF16, OFF_B + 16384 + i * 32768) for i in range(2)]
    attnTM = [sb(f"attnTM{j}", [128, 4096], F32, OFF_B + j * 16384) for j in range(4)]
    attnTMall = sb("attnTMall", [128, 16384], F32, OFF_B)
    shT = sb("shT", [128, 16384], BF16, OFF_B)
    act = [sb(f"act{e}", [128, 16384], BF16,
              [OFF_A, OFF_A + 32768, OFF_B + 32768][e]) for e in range(3)]
    smtm = [sb(f"smtm{j}", [128, 4096], F32, OFF_A + j * 16384) for j in range(4)]
    smtmall = sb("smtmall", [128, 16384], F32, OFF_A)

    wpan = [sb(f"wpan{i}", [128, 4096], BF16,
               [OFF_WPAN, OFF_WPAN + 8192, OFF_GRAW + 8192][i]) for i in range(3)]
    x1out = sb("x1out", [128, 4096], BF16, OFF_WPAN)
    h16t = sb("h16t", [128, 4096], BF16, OFF_GRAW)
    cosb = sb("cosb", [128, 1024], F32, OFF_COS)
    sinb = sb("sinb", [128, 1024], F32, OFF_COS + 4096)
    swoc = [sb(f"swoc{i}", [128, 512], F32, OFF_COS + i * 2048) for i in range(2)]
    graw = sb("graw", [128, 8192], BF16, OFF_GRAW)

    xt = [sb(f"xt{i}", [128, 4096], F32, OFF_E + i * 16384) for i in range(2)]
    ropet = sb("ropet", [128, 1024], F32, OFF_E)
    rope1 = sb("rope1", [128, 1024], F32, OFF_E + 4096)
    ptile = [sb(f"ptile{i}", [128, 1024], BF16, OFF_E + i * 2048) for i in range(2)]
    sumV = [sb(f"sumV{i}", [128, 1024], F32, [OFF_E + 4096, OFF_E + 16384][i]) for i in range(2)]
    sumO = [sb(f"sumO{i}", [128, 1024], F32, [OFF_E + 12288, OFF_E + 20480][i]) for i in range(2)]
    sbcast = sb("sbcast", [128, 1024], F32, OFF_E + 8192)
    xtile = sb("xtile", [128, 4096], F32, OFF_E)
    htile = sb("htile", [128, 4096], F32, OFF_E + 16384)
    sce = [sb(f"sce{e}", [128, 512], F32, OFF_E + e * 2048) for e in range(3)]
    smoec = sb("smoec", [128, 512], F32, OFF_E + 6144)
    mstg = sb("mstg", [128, 512], F32, OFF_E + 8192)
    sactf = sb("sactf", [128, 512], F32, OFF_E + 10240)
    a0bc = sb("a0bc", [128, 512], F32, OFF_E + 12288)
    a1bc = sb("a1bc", [128, 512], F32, OFF_E + 14336)
    x2t = sb("x2t", [128, 4096], F32, OFF_E + 16384)

    fo = [OFF_F]

    def falloc(name, shape, dtype):
        t = sb(name, shape, dtype, fo[0])
        bpp = int(np.prod(shape[1:])) * (2 if dtype == BF16 else 4)
        fo[0] += (bpp + 63) // 64 * 64
        return t

    eye16 = falloc("eye16", [128, 128], BF16)
    eye32 = falloc("eye32", [128, 128], F32)
    ones16 = falloc("ones16", [1, 128], BF16)
    ones32 = falloc("ones32", [1, 128], F32)
    sgw = falloc("sgw", [128, 96], BF16)
    sb1t = falloc("sb1t", [128, 96], F32)
    sb2t = falloc("sb2t", [128, 96], F32)
    b2c = [falloc(f"b2c{i}", [128, 32], F32) for i in range(2)]
    hbc = [falloc(f"hbc{i}", [128, 4], F32) for i in range(2)]
    smv = falloc("smv", [128, 8], F32)
    smt = falloc("smt", [128, 8], F32)
    shs = falloc("shs", [128, 8], F32)
    sbt = falloc("sbt", [128, 4], I32)
    sidxK = falloc("sidxK", [128, 32], I32)
    sidxV = falloc("sidxV", [128, 32], I32)
    sbn = falloc("sbn", [128, 96], F32)
    sag = falloc("sag", [128, 2], F32)
    snm = falloc("snm", [128, 1], F32)
    srs = falloc("srs", [128, 1], F32)
    srs2 = falloc("srs2", [128, 1], F32)
    ssums = falloc("ssums", [128, 8], F32)
    srec = falloc("srec", [128, 8], F32)
    stt1 = falloc("stt1", [128, 8], F32)
    sgt = [falloc(f"sgt{j}", [128, 4], F32) for j in range(4)]
    sa0 = falloc("sa0", [128, 1], F32)
    sa1 = falloc("sa1", [128, 1], F32)
    sgx = falloc("sgx", [128, 4], F32)
    sgs1 = falloc("sgs1", [128, 1], F32)
    sgs2 = falloc("sgs2", [128, 1], F32)
    sgs3 = falloc("sgs3", [128, 1], F32)
    srecro = sb("srecro", [1, 1024], F32, OFF_COS)
    a0row = sb("a0row", [1, 512], F32, OFF_COS)
    a1row = sb("a1row", [1, 512], F32, OFF_COS + 2048)
    sglog = sb("sglog", [4, 512], F32, OFF_COS + 4096)
    spay = falloc("spay", [1, 8], F32)
    spart = falloc("spart", [1, 8], F32)
    srow1 = falloc("srow1", [1, 8], F32)
    srow2 = falloc("srow2", [1, 8], F32)
    sg0 = falloc("sg0", [1, 4], F32)
    scv = falloc("scv", [1, 4], F32)
    sh0 = falloc("sh0", [1, 4], F32)
    sh1 = falloc("sh1", [1, 4], F32)
    scvb = falloc("scvb", [128, 4], F32)
    sgwl = falloc("sgwl", [1, 4], F32)
    sgbr = falloc("sgbr", [1, 4], F32)
    spar = falloc("spar", [1, 1], I32)
    sparf = falloc("sparf", [1, 1], F32)
    stag = falloc("stag", [1, 1], I32)
    stagp = falloc("stagp", [1, 1], I32)
    assert fo[0] < SB_LIMIT, fo[0]

    PS = nc.alloc_psum_tensor("PS", [128, 4096], F32)

    def ps(a, b):
        return PS[:, a:b]

    P = Plan()
    P.nc = nc
    dMISC = Ctr(nc, "dMISC")
    dGM = Ctr(nc, "dGM")
    dX = [Ctr(nc, "dX0"), Ctr(nc, "dX1")]
    dXW = Ctr(nc, "dXW")
    dT = Ctr(nc, "dT")
    dS = [Ctr(nc, "dS0"), Ctr(nc, "dS1")]
    dKV = Ctr(nc, "dKV")
    dW = [Ctr(nc, "dW0"), Ctr(nc, "dW1"), Ctr(nc, "dW2")]
    dXT = Ctr(nc, "dXT")
    dX2W = Ctr(nc, "dX2W")
    dHW = Ctr(nc, "dHW")
    dMW = Ctr(nc, "dMW")
    dMS = Ctr(nc, "dMS")
    dX2 = Ctr(nc, "dX2")
    dOUT = Ctr(nc, "dOUT")
    cPE = Ctr(nc, "cPE")
    cSC = Ctr(nc, "cSC")
    cVE = Ctr(nc, "cVE")
    cGP = Ctr(nc, "cGP")
    P.self_ctr = {"vector": cVE, "scalar": cSC, "gpsimd": cGP}
    swx = Ctr(nc, "swx")
    sdump = Ctr(nc, "sdump")
    spx = Ctr(nc, "spx")

    slot_pe = {0: 0, 1: 0, 2: 0}

    def dbg_dump(name, ap_fn, waits):
        if name in dbg_h:
            h = dbg_h[name]
            P.dma("gpsimd", dGM, lambda e, h=h, f=ap_fn: e.dma_start(out=h.ap(), in_=f()), waits)

    P.phase = 'p0'
    for fn in [
        lambda e: e.dma_start(out=sbt[:, :], in_=bt_h.ap().rearrange("(g p) -> p g", p=128), allow_slow_non_contiguous=True),
        lambda e: e.dma_start(out=eye16[:, :], in_=eye16_h.ap()),
        lambda e: e.dma_start(out=eye32[:, :], in_=eye32_h.ap()),
        lambda e: e.dma_start(out=ones16[:, :], in_=ones16_h.ap()),
        lambda e: e.dma_start(out=ones32[:, :], in_=ones32_h.ap()),
        lambda e: e.dma_start(out=sb1t[:, :], in_=b1t_h.ap()),
        lambda e: e.dma_start(out=sb2t[:, :], in_=b2t_h.ap()),
        lambda e: e.dma_start(out=cosb[:, :], in_=cos_h.ap()),
        lambda e: e.dma_start(out=sinb[:, :], in_=sin_h.ap()),
        lambda e: e.dma_start(out=smv[:, :], in_=mv_h.ap()),
        lambda e: e.dma_start(out=smt[:, :], in_=mt_h.ap()),
        lambda e: e.dma_start(out=sgwl[:, :], in_=gwl_h.ap()),
        lambda e: e.dma_start(out=sgbr[:, :], in_=gb_h.ap()),
        lambda e: e.dma_start(out=spar[:, :], in_=par_h.ap()),
        lambda e: e.dma_start(out=sparf[:, :], in_=parf_h.ap()),
        lambda e: e.dma_start(out=stag[:, :], in_=tag_h.ap()),
    ]:
        P.dma("sync", dMISC, fn)
    P.dma("gpsimd", dGM, lambda e: e.dma_start(out=sgw[:, :].rearrange("p (g e) -> p g e", e=3), in_=gw_h.ap().rearrange("g p e -> p g e")))
    p0w = dMISC.n
    p0g = dGM.n
    for kvh in range(8):
        P.c("vector", cVE, lambda e, k=kvh: e.tensor_scalar(
            sidxK[:, k * 4:(k + 1) * 4], sbt[:, :], 16, k, ALU.mult, ALU.add),
            waits=[W(dMISC, p0w)])
        P.c("vector", cVE, lambda e, k=kvh: e.tensor_scalar(
            sidxV[:, k * 4:(k + 1) * 4], sbt[:, :], 16, 8 + k, ALU.mult, ALU.add))
    idx_done = cVE.n

    def ln_stats(src_ap, waits):
        def f1(e):
            src = src_ap()
            r = None
            for i in range(8):
                r = e.bn_stats(sbn[:, i * 6:i * 6 + 6], src[:, i * 512:(i + 1) * 512])
            return r
        P.c("vector", cVE, f1, waits=waits)
        P.c("vector", cVE, lambda e: e.bn_aggr(sag[:, :], sbn[:, 0:48]))
        def f2(e):
            e.tensor_scalar(srs[:, :], sag[:, 1:2], EPS, None, ALU.add)
            return e.tensor_scalar(snm[:, :], sag[:, 0:1], -1.0, None, ALU.mult)
        P.c("vector", cVE, f2)
        P.c("scalar", cSC, lambda e: e.activation(srs2[:, :], srs[:, :], AF.Sqrt),
            waits=[W(cVE, None)])
        P.c("vector", cVE, lambda e: e.reciprocal(srs[:, :], srs2[:, :]), waits=[W(cSC, None)])

    P.phase = 'p1a'
    ve_x1 = []
    x1w = []
    for st in range(8):
        buf = xt[st % 2]
        dc = dX[st % 2]
        w = [W(cVE, ve_x1[st - 2])] if st >= 2 else []
        P.dma("sync", dc, lambda e, b=buf, s=st: e.dma_start(
            out=b[:, :], in_=x_h.ap()[s * 128:(s + 1) * 128, :]), w)
        ln_stats((lambda b=buf: b[:, :]), waits=[W(dc, None)])
        w = [W(cVE, None)]
        if st >= 1:
            w.append(W(dXW, x1w[st - 1]))
        P.c("vector", cVE, lambda e, b=buf: e.tensor_scalar(
            x1out[:, :], b[:, :], snm[:, 0:1], srs[:, 0:1], ALU.add, ALU.mult), waits=w)
        ve_x1.append(cVE.n)
        P.dma("sync", dXW, lambda e, s=st: e.dma_start(
            out=x1scr.ap()[s * 128:(s + 1) * 128, :], in_=x1out[:, :]), [W(cVE, None)])
        x1w.append(dXW.n)
    x1w_done = dXW.n
    dbg_dump("dbg_x1", lambda: x1scr.ap()[0:128, :], [W(dXW, x1w_done)])

    if stop_after == "p1a":
        for st in range(8):
            P.dma("gpsimd", dGM, lambda e, s=st: e.dma_start(
                out=out_h.ap()[s * 128:(s + 1) * 128, :],
                in_=x1scr.ap()[s * 128:(s + 1) * 128, :]), [W(dXW, x1w_done)])
        P.raw("gpsimd", lambda e: e.wait_ge(dGM.sem, dGM.n))
        with nc.Block() as block:
            P.replay(block)
        return nc

    P.phase = 'p1b'
    for fc in range(32):
        P.dma("sync", dT, lambda e, f=fc: e.dma_start(
            out=x1T[:, f * 1024:(f + 1) * 1024],
            in_=x1scr.ap()[:, f * 128:(f + 1) * 128], transpose=True),
            [W(dXW, x1w_done)])
    x1T_done = dT.n
    dbg_dump("dbg_x1T", lambda: x1T[:, 0:1024], [W(dT, x1T_done)])

    pe_q = []
    ve_rope = []
    sc_rope = []
    for qi in range(32):
        slot = qi % 2
        wq_w = [W(cPE, slot_pe[slot])]
        if qi < 2:
            wq_w.append(W(dXW, x1w_done))
        P.dma("gpsimd", dW[slot], lambda e, s=slot, q=qi: e.dma_start(
            out=wpan[s][:, :], in_=wq_h.ap()[q, :, :]), wq_w)

        def qmms(e, s=slot, q=qi):
            r = []
            b = s * 1024
            for fi in range(32):
                for hh in range(2):
                    r.append(e.matmul(
                        ps(b + hh * 512, b + hh * 512 + 512),
                        wpan[s][:, fi * 128:(fi + 1) * 128],
                        x1T[:, fi * 1024 + hh * 512: fi * 1024 + hh * 512 + 512],
                        start=(fi == 0), stop=(fi == 31)))
            return r
        mmw = [W(dW[slot], None), W(dT, x1T_done)]
        if qi >= 2:
            mmw += [W(cVE, ve_rope[qi - 2]), W(cSC, sc_rope[qi - 2])]
        P.c("tensor", cPE, qmms, waits=mmw)
        pe_q.append(cPE.n)
        slot_pe[slot] = cPE.n
        b = slot * 1024

        def ropes(e, b=b):
            e.mul(ropet[0:64, :], PS[64:128, b:b + 1024], -1.0)
            return e.copy(ropet[64:128, :], PS[0:64, b:b + 1024])
        P.c("scalar", cSC, ropes, waits=[W(cPE, None)])
        sc_rope.append(cSC.n)

        def ropev(e, b=b):
            e.tensor_tensor(rope1[:, :], PS[:, b:b + 1024], cosb[:, :], ALU.mult)
            return e.tensor_tensor(ropet[:, :], ropet[:, :], sinb[:, :], ALU.mult)
        P.c("vector", cVE, ropev, waits=[W(cSC, None)])
        P.c("vector", cVE, lambda e, q=qi: e.tensor_tensor(
            qctx[:, q * 1024:(q + 1) * 1024], rope1[:, :], ropet[:, :], ALU.add))
        ve_rope.append(cVE.n)
    q_done_pe = cPE.n
    q_done_ve = cVE.n

    dbg_dump("dbg_q", lambda: qctx[:, 0:4096], [W(cVE, q_done_ve)])

    P.phase = 'p1c'
    prep_rK, prep_rV, kt_read, v_read = [], [], [], []
    comp_done = []
    ve_ctx_done = []

    kvrows = kv_h.ap().rearrange("a b c d e -> (a b c) (d e)")

    def plan_prep(j):
        slq = j % 2
        dc = dS[slq]
        for g in range(4):
            w = [W(cVE, idx_done)]
            if j > 0:
                w.append(W(dS[(j - 1) % 2], prep_rV[j - 1]))
            P.dma("gpsimd", dKV, lambda e, g=g, k=j: e.indirect_dma_start(
                graw[:, g * 2048:(g + 1) * 2048], None, kvrows,
                IndirectOffsetOnAxis(sidxK[:, k * 4 + g:k * 4 + g + 1], 0)), w)
        w = [W(dKV, None)]
        if j >= 2:
            w.append(W(dc, kt_read[j - 2]))
        P.dma("sync", dc, lambda e, s=slq: e.dma_start(
            out=kvlK[s].ap().rearrange("(g p j) d -> p g j d", g=4, p=128, j=16),
            in_=graw[:, :].rearrange("p (g j d) -> p g j d", g=4, j=16, d=128)), w)
        prep_rK.append(dc.n)
        for g in range(4):
            P.dma("gpsimd", dKV, lambda e, g=g, k=j: e.indirect_dma_start(
                graw[:, g * 2048:(g + 1) * 2048], None, kvrows,
                IndirectOffsetOnAxis(sidxV[:, k * 4 + g:k * 4 + g + 1], 0)),
                [W(dc, prep_rK[j])])
        w = [W(dKV, None)]
        if j >= 2:
            w.append(W(dc, v_read[j - 2]))
        P.dma("sync", dc, lambda e, s=slq: e.dma_start(
            out=kvlV[s].ap().rearrange("(g p j) d -> p g j d", g=4, p=128, j=16),
            in_=graw[:, :].rearrange("p (g j d) -> p g j d", g=4, j=16, d=128)), w)
        prep_rV.append(dc.n)
        w = [W(dc, prep_rK[j]), W(cPE, q_done_pe)]
        if j >= 2:
            w.append(W(cPE, comp_done[j - 2]))
        P.dma("sync", dc, lambda e, s=slq: e.dma_start(
            out=KT[s][:, :], in_=kvlK[s].ap(), transpose=True), w)
        kt_read.append(dc.n)
        w = [W(dc, prep_rV[j]), W(cPE, q_done_pe)]
        if j >= 2:
            w.append(W(cPE, comp_done[j - 2]))
        P.dma("sync", dc, lambda e, s=slq: e.dma_start(
            out=Vb[s][:, :].rearrange("p (tt d) -> p tt d", d=128),
            in_=kvlV[s].ap().rearrange("(tt p) d -> p tt d", p=128)), w)
        v_read.append(dc.n)

    plan_prep(0)
    for j in range(8):
        if j + 1 < 8:
            plan_prep(j + 1)
        slq = j % 2
        for qh in range(4):
            h = j * 4 + qh
            hp = h % 2
            pe_scores, pe_ctx, sums, sc_exp = [], [], [], []

            def emit_scores(tt):
                par = tt % 2
                sbk = par * 1024

                def scmm(e, s=slq, t=tt, hh=h, b=sbk):
                    r = []
                    for half in range(2):
                        r.append(e.matmul(
                            ps(b + half * 512, b + half * 512 + 512),
                            KT[s][:, t * 128:(t + 1) * 128],
                            qctx[:, hh * 1024 + half * 512: hh * 1024 + half * 512 + 512],
                            start=True, stop=True))
                    return r
                w = [W(dS[slq], kt_read[j]), W(dS[slq], v_read[j]), W(cVE, q_done_ve)]
                if tt >= 2:
                    w.append(W(cSC, sc_exp[tt - 2]))
                elif h > 0:
                    w.append(W(cSC, prev_exp[62 + tt]))
                P.c("tensor", cPE, scmm, waits=w)
                pe_scores.append(cPE.n)

                w = [W(cPE, pe_scores[tt])]
                if tt >= 2:
                    w += [W(cPE, pe_ctx[tt - 2]), W(sums[tt - 2][0], sums[tt - 2][1])]
                elif h > 0:
                    w += [W(cPE, prev_ctx[62 + tt]),
                          W(prev_sums[62 + tt][0], prev_sums[62 + tt][1])]
                P.c("scalar", cSC, lambda e, p=par, b=sbk: e.activation(
                    ptile[p][:, :], ps(b, b + 1024), AF.Exp), waits=w)
                sc_exp.append(cSC.n)

            def emit_ctx(tt):
                par = tt % 2

                def ctmm(e, s=slq, t=tt, p=par):
                    r = []
                    for half in range(2):
                        r.append(e.matmul(
                            ps(2048 + half * 512, 2048 + half * 512 + 512),
                            Vb[s][:, t * 128:(t + 1) * 128],
                            ptile[p][:, half * 512:half * 512 + 512],
                            start=(t == 0), stop=(t == 63)))
                    return r
                w = [W(cSC, sc_exp[tt])]
                if tt == 0 and h > 0:
                    w.append(W(cVE, None))
                P.c("tensor", cPE, ctmm, waits=w)
                pe_ctx.append(cPE.n)

                pool = (tt % 8) < 3
                eng = "gpsimd" if pool else "vector"
                ctr = cGP if pool else cVE
                acc = (sumO if pool else sumV)[hp]
                first = (tt < 8) and ((tt % 8) == (0 if pool else 3))

                def sfn(e, p=par, a=acc, f=first):
                    if f:
                        return e.tensor_copy(a[:, :], ptile[p][:, :])
                    return e.tensor_tensor(a[:, :], a[:, :], ptile[p][:, :], ALU.add)
                sw = [W(cSC, sc_exp[tt])]
                if tt < 8 and h >= 2:
                    sw.append(W(cVE, None))
                P.c(eng, ctr, sfn, waits=sw)
                sums.append((ctr, ctr.n))

            def emit_epilogue():
                hq = h - 1
                hqp = hq % 2
                P.c("vector", cVE, lambda e, a=sumV[hqp], b=sumO[hqp]: e.tensor_tensor(
                    a[:, :], a[:, :], b[:, :], ALU.add),
                    waits=[W(prev_sums[63][0], prev_sums[63][1]),
                           W(prev_sums[62][0], prev_sums[62][1]),
                           W(cGP, None)])
                def tr8(e, a=sumV[hqp]):
                    r = []
                    for k in range(8):
                        r.append(e.transpose(PS[:, 3072 + k * 128:3200 + k * 128],
                                             a[:, k * 128:(k + 1) * 128], eye32[:, :]))
                    return r
                P.c("tensor", cPE, tr8, waits=[W(cVE, None)])
                P.c("vector", cVE, lambda e: e.reduce_sum(
                    ssums[:, 0:8], PS[:, 3072:4096].rearrange("p (k f) -> p k f", k=8), AX.X),
                    waits=[W(cPE, None)])
                P.c("vector", cVE, lambda e: e.reciprocal(srec[:, 0:8], ssums[:, 0:8]))

                def tr1(e):
                    r = []
                    for k in range(8):
                        r.append(e.transpose(PS[0:1, 3072 + k * 128:3072 + k * 128 + 128],
                                             srec[:, k:k + 1], eye32[:, :]))
                    return r
                P.c("tensor", cPE, tr1, waits=[W(cVE, None)])
                P.c("vector", cVE, lambda e: e.tensor_copy(srecro[0:1, :], PS[0:1, 3072:4096]),
                    waits=[W(cPE, None)])

                def bcmm(e):
                    r = []
                    for half in range(2):
                        r.append(e.matmul(ps(3072 + half * 512, 3072 + half * 512 + 512),
                                          ones32[:, :],
                                          srecro[0:1, half * 512:half * 512 + 512],
                                          start=True, stop=True))
                    return r
                P.c("tensor", cPE, bcmm, waits=[W(cVE, None)])
                P.c("vector", cVE, lambda e: e.tensor_copy(sbcast[:, :], ps(3072, 4096)),
                    waits=[W(cPE, None)])
                P.c("vector", cVE, lambda e, hh=hq: e.tensor_tensor(
                    qctx[:, hh * 1024:(hh + 1) * 1024], ps(2048, 3072), sbcast[:, :], ALU.mult),
                    waits=[W(cPE, prev_ctx[63])])
                ve_ctx_done.append(cVE.n)

            emit_scores(0)
            emit_scores(1)
            if h > 0:
                emit_epilogue()
            emit_ctx(0)
            for tt in range(2, 64):
                emit_scores(tt)
                emit_ctx(tt - 1)
            emit_ctx(63)
            prev_exp = sc_exp
            prev_ctx = pe_ctx
            prev_sums = sums
        comp_done.append(cPE.n)
    hqp = 31 % 2
    P.c("vector", cVE, lambda e, a=sumV[hqp], b=sumO[hqp]: e.tensor_tensor(
        a[:, :], a[:, :], b[:, :], ALU.add),
        waits=[W(prev_sums[63][0], prev_sums[63][1]),
               W(prev_sums[62][0], prev_sums[62][1]), W(cGP, None)])

    def tr8f(e, a=sumV[hqp]):
        r = []
        for k in range(8):
            r.append(e.transpose(PS[:, 3072 + k * 128:3200 + k * 128],
                                 a[:, k * 128:(k + 1) * 128], eye32[:, :]))
        return r
    P.c("tensor", cPE, tr8f, waits=[W(cVE, None)])
    P.c("vector", cVE, lambda e: e.reduce_sum(
        ssums[:, 0:8], PS[:, 3072:4096].rearrange("p (k f) -> p k f", k=8), AX.X),
        waits=[W(cPE, None)])
    P.c("vector", cVE, lambda e: e.reciprocal(srec[:, 0:8], ssums[:, 0:8]))

    def tr1f(e):
        r = []
        for k in range(8):
            r.append(e.transpose(PS[0:1, 3072 + k * 128:3072 + k * 128 + 128],
                                 srec[:, k:k + 1], eye32[:, :]))
        return r
    P.c("tensor", cPE, tr1f, waits=[W(cVE, None)])
    P.c("vector", cVE, lambda e: e.tensor_copy(srecro[0:1, :], PS[0:1, 3072:4096]),
        waits=[W(cPE, None)])

    def bcmmf(e):
        r = []
        for half in range(2):
            r.append(e.matmul(ps(3072 + half * 512, 3072 + half * 512 + 512),
                              ones32[:, :], srecro[0:1, half * 512:half * 512 + 512],
                              start=True, stop=True))
        return r
    P.c("tensor", cPE, bcmmf, waits=[W(cVE, None)])
    P.c("vector", cVE, lambda e: e.tensor_copy(sbcast[:, :], ps(3072, 4096)),
        waits=[W(cPE, None)])
    P.c("vector", cVE, lambda e: e.tensor_tensor(
        qctx[:, 31 * 1024:32 * 1024], ps(2048, 3072), sbcast[:, :], ALU.mult),
        waits=[W(cPE, prev_ctx[63])])
    ve_ctx_done.append(cVE.n)
    ctx_all_done = cVE.n

    dbg_dump("dbg_ctx", lambda: qctx[:, 0:4096], [W(cVE, ctx_all_done)])

    P.phase = 'p1d'
    hs_written = []
    x2_written = []
    last_wo_pe = q_done_pe
    for sh in range(2):
        pe_wo, ve_piece = [], []
        for oi in range(32):
            slot = oi % 3
            wv = [W(cPE, slot_pe[slot])]
            if sh == 0 and oi == 2:
                wv.append(W(dS[1], prep_rV[7]))
            P.dma("gpsimd", dW[slot], lambda e, s=slot, o=oi: e.dma_start(
                out=wpan[s][:, :], in_=wo_h.ap()[o, :, :]), wv)

            def womm(e, s=slot, b=(oi % 2) * 512, half=sh):
                r = []
                for fi in range(32):
                    r.append(e.matmul(
                        ps(b, b + 512), wpan[s][:, fi * 128:(fi + 1) * 128],
                        qctx[:, fi * 1024 + half * 512: fi * 1024 + half * 512 + 512],
                        start=(fi == 0), stop=(fi == 31)))
                return r
            w = [W(dW[slot], None), W(cVE, ctx_all_done)]
            if oi >= 2:
                w.append(W(cVE, ve_piece[oi - 2]))
            P.c("tensor", cPE, womm, waits=w)
            pe_wo.append(cPE.n)
            slot_pe[slot] = cPE.n
            P.c("vector", cVE, lambda e, b=(oi % 2) * 512, s=oi % 2: e.tensor_copy(
                swoc[s][:, :], ps(b, b + 512)), waits=[W(cPE, None)])
            evd = cVE.n

            def wotr(e, s=oi % 2):
                r = []
                for k in range(4):
                    r.append(e.transpose(PS[:, 3072 + k * 128:3072 + k * 128 + 128],
                                         swoc[s][:, k * 128:(k + 1) * 128], eye32[:, :]))
                return r
            w = [W(cVE, evd)]
            if ve_piece:
                w.append(W(cVE, ve_piece[-1]))
            P.c("tensor", cPE, wotr, waits=w)
            pw = [W(cPE, None)]
            if sh == 1 and oi == 0:
                pw.append(W(dHW, hs_written[3]))
                pw.append(W(dX2W, x2_written[3]))
            P.c("vector", cVE, lambda e, o=oi: e.tensor_copy(
                attnTMall[:, :].rearrange("p (k f) -> p k f", k=4)[:, :, o * 128:(o + 1) * 128],
                PS[:, 3072:3584].rearrange("p (k f) -> p k f", k=4)), waits=pw)
            ve_piece.append(cVE.n)
        wo_half_done = cVE.n
        last_wo_pe = cPE.n

        for sj in range(4):
            st = sh * 4 + sj
            ln_stats((lambda k=sj: attnTM[k][:, :]), waits=[W(cVE, wo_half_done)])
            P.c("vector", cVE, lambda e, k=sj: e.tensor_scalar(
                attnTM[k][:, :], attnTM[k][:, :], snm[:, 0:1], srs[:, 0:1], ALU.add, ALU.mult),
                waits=[W(cVE, None)])
            P.dma("sync", dXT, lambda e, s=st: e.dma_start(
                out=xtile[:, :], in_=x_h.ap()[s * 128:(s + 1) * 128, :]), [W(cVE, None)])
            P.c("vector", cVE, lambda e, k=sj: e.tensor_tensor(
                attnTM[k][:, :], attnTM[k][:, :], xtile[:, :], ALU.add), waits=[W(dXT, None)])
            x2done = cVE.n
            P.dma("sync", dX2W, lambda e, s=st, k=sj: e.dma_start(
                out=x2scr.ap()[s * 128:(s + 1) * 128, :], in_=attnTM[k][:, :]),
                [W(cVE, x2done)])
            x2_written.append(dX2W.n)
            ln_stats((lambda k=sj: attnTM[k][:, :]), waits=[W(cVE, x2done)])
            P.c("vector", cVE, lambda e, k=sj: e.tensor_scalar(
                htile[:, :], attnTM[k][:, :], snm[:, 0:1], srs[:, 0:1], ALU.add, ALU.mult),
                waits=[W(cVE, None)])
            P.c("vector", cVE, lambda e, s=st: e.reduce_sum(
                shs[:, s:s + 1], htile[:, :], AX.X, apply_absolute_value=True),
                waits=[W(cVE, None)])
            w = [W(cVE, None)]
            if sh == 0 and sj == 0:
                w.append(W(dS[1], prep_rV[7]))
            else:
                w.append(W(dHW, hs_written[-1]))
            P.c("vector", cVE, lambda e: e.tensor_copy(h16t[:, :], htile[:, :]), waits=w)
            P.dma("sync", dHW, lambda e, s=st: e.dma_start(
                out=hscr.ap()[s * 128:(s + 1) * 128, :], in_=h16t[:, :]), [W(cVE, None)])
            hs_written.append(dHW.n)
    p1d_ve = cVE.n
    dbg_dump("dbg_h", lambda: hscr.ap()[0:128, :], [W(dHW, hs_written[0])])

    dbg_dump("dbg_x2", lambda: attnTM[3][:, :], [W(cVE, p1d_ve)])

    P.phase = 'p2'
    for fc in range(32):
        P.dma("sync", dT, lambda e, f=fc: e.dma_start(
            out=shT[:, f * 512:(f + 1) * 512],
            in_=hscr.ap()[0:512, f * 128:(f + 1) * 128], transpose=True),
            [W(dHW, hs_written[3]), W(cVE, p1d_ve)])
    shT0_done = dT.n

    def gatemm(e):
        r = []
        for fi in range(32):
            r.append(e.matmul(PS[0:3, 3072:3584], sgw[:, fi * 3:(fi + 1) * 3],
                              shT[:, fi * 512:(fi + 1) * 512],
                              start=(fi == 0), stop=(fi == 31)))
        return r
    P.c("tensor", cPE, gatemm, waits=[W(dT, shT0_done), W(dGM, p0g), W(cVE, p1d_ve)])
    P.c("vector", cVE, lambda e: e.tensor_copy(sglog[0:3, :], PS[0:3, 3072:3584]),
        waits=[W(cPE, None)])
    P.c("tensor", cPE, lambda e: e.transpose(PS[0:1, 3584:3587], sglog[0:3, 0:1],
                                             eye32[0:3, 0:3]), waits=[W(cVE, None)])
    P.c("vector", cVE, lambda e: e.tensor_copy(srow1[0:1, 0:3], PS[0:1, 3584:3587]),
        waits=[W(cPE, None)])

    P.c("vector", cVE, lambda e: e.tensor_tensor(stt1[:, :], shs[:, :], smv[:, :], ALU.mult),
        waits=[W(cVE, p1d_ve)])
    P.c("vector", cVE, lambda e: e.reduce_sum(sgs1[:, :], stt1[:, :], AX.X))
    P.c("vector", cVE, lambda e: e.tensor_tensor(stt1[:, :], shs[:, :], smt[:, :], ALU.mult))
    P.c("vector", cVE, lambda e: e.reduce_sum(sgs2[:, :], stt1[:, :], AX.X))
    P.c("tensor", cPE, lambda e: e.transpose(PS[0:1, 3600:3728], sgs1[:, 0:1], eye32[:, :]),
        waits=[W(cVE, None)])
    P.c("vector", cVE, lambda e: e.reduce_sum(spay[0:1, 0:1], PS[0:1, 3600:3728], AX.X),
        waits=[W(cPE, None)])
    P.c("tensor", cPE, lambda e: e.transpose(PS[0:1, 3600:3728], sgs2[:, 0:1], eye32[:, :]),
        waits=[W(cVE, None)])
    P.c("vector", cVE, lambda e: e.reduce_sum(spay[0:1, 1:2], PS[0:1, 3600:3728], AX.X),
        waits=[W(cPE, None)])
    P.c("vector", cVE, lambda e: e.tensor_copy(spay[0:1, 2:5], srow1[0:1, 0:3]))
    pay_done = cVE.n

    def exchange(e):
        par_r = e.alloc_register("par_r")
        e.reg_load(par_r, spar[0:1, 0:1])
        pnot_r = e.alloc_register("pnot_r")
        e.reg_alu(pnot_r, par_r, 1, ALU.bitwise_xor)
        parv = e.snap(par_r, min_val=0, max_val=1)
        pnotv = e.snap(pnot_r, min_val=0, max_val=1)
        e.dma_start(out=shd_h.ap()[bass.ts(parv, 1), :], in_=spay[0:1, :]).then_inc(swx.sem, 16)
        e.wait_ge(swx.sem, 16)
        e.dma_start(out=sht_h.ap()[bass.ts(parv, 1), :], in_=stag[0:1, :],
                    allow_slow_non_contiguous=True).then_inc(swx.sem, 16)
        texp_r = e.alloc_register("texp_r")
        e.reg_load(texp_r, stag[0:1, 0:1])
        tcur_r = e.alloc_register("tcur_r")
        nm_r = e.alloc_register("nm_r")
        e.reg_mov(nm_r, 1)
        with e.While(nm_r):
            e.dma_start(out=stagp[:, :], in_=sht_h.ap()[bass.ts(pnotv, 1), 0:1],
                        allow_slow_non_contiguous=True).then_inc(sdump.sem, 16)
            e.reg_load(tcur_r, stagp[0:1, 0:1])
            e.reg_sub(nm_r, tcur_r, texp_r)
        r = e.dma_start(out=spart[0:1, :], in_=shd_h.ap()[bass.ts(pnotv, 1), :])
        return r
    P.c("sync", None, lambda e: exchange(e).then_inc(spx.sem, 16) and e.nop(),
        waits=[W(cVE, pay_done)])

    EXW = [W(spx, 16), W(cVE, pay_done)]
    P.c("vector", cVE, lambda e: e.tensor_tensor(
        srow2[0:1, 0:2], spay[0:1, 0:2], spart[0:1, 0:2], ALU.add), waits=EXW)
    P.c("vector", cVE, lambda e: e.tensor_tensor(
        srow1[0:1, 0:3], spart[0:1, 2:5], spay[0:1, 2:5], ALU.subtract))
    P.c("vector", cVE, lambda e: e.tensor_scalar(
        srow1[0:1, 0:3], srow1[0:1, 0:3], sparf[0:1, 0:1], None, ALU.mult))
    P.c("vector", cVE, lambda e: e.tensor_tensor(
        sg0[0:1, 0:3], spay[0:1, 2:5], srow1[0:1, 0:3], ALU.add))
    P.c("vector", cVE, lambda e: e.tensor_scalar(
        sgs3[0:1, 0:1], srow2[0:1, 1:2], 1.0 / ((SEQ - VIS) * D), None, ALU.mult))
    P.c("vector", cVE, lambda e: e.tensor_scalar(
        srow1[0:1, 0:1], srow2[0:1, 0:1], 1.0 / (VIS * D), 1e-8, ALU.mult, ALU.add))
    P.c("vector", cVE, lambda e: e.reciprocal(srow2[0:1, 2:3], srow1[0:1, 0:1]))
    P.c("vector", cVE, lambda e: e.tensor_tensor(
        sgs3[0:1, 0:1], sgs3[0:1, 0:1], srow2[0:1, 2:3], ALU.mult))
    P.c("vector", cVE, lambda e: e.tensor_scalar(
        scv[0:1, 0:3], sgwl[0:1, 0:3], sgs3[0:1, 0:1], None, ALU.mult))
    P.c("vector", cVE, lambda e: e.tensor_tensor(
        scv[0:1, 0:3], scv[0:1, 0:3], sgbr[0:1, 0:3], ALU.add))
    P.c("vector", cVE, lambda e: e.tensor_tensor(
        sg0[0:1, 0:3], sg0[0:1, 0:3], scv[0:1, 0:3], ALU.add))
    P.c("vector", cVE, lambda e: e.reduce_max(srow1[0:1, 0:1], sg0[0:1, 0:3], AX.X))
    P.c("vector", cVE, lambda e: e.tensor_reduce(srow1[0:1, 1:2], sg0[0:1, 0:3], AX.X, ALU.min))
    P.c("vector", cVE, lambda e: e.tensor_scalar(
        sh0[0:1, 0:3], sg0[0:1, 0:3], srow1[0:1, 0:1], None, ALU.is_ge))
    P.c("vector", cVE, lambda e: e.tensor_scalar(
        sh1[0:1, 0:3], sg0[0:1, 0:3], srow1[0:1, 1:2], None, ALU.is_le))
    P.c("vector", cVE, lambda e: e.tensor_tensor(
        sh1[0:1, 0:3], sh1[0:1, 0:3], sh0[0:1, 0:3], ALU.add))
    P.c("vector", cVE, lambda e: e.tensor_scalar(
        sh1[0:1, 0:3], sh1[0:1, 0:3], -1.0, 1.0, ALU.mult, ALU.add))
    router_ve = cVE.n
    P.c("tensor", cPE, lambda e: e.matmul(PS[:, 3587:3590], ones32[:, :], scv[0:1, 0:3],
                                          start=True, stop=True), waits=[W(cVE, router_ve)])
    P.c("vector", cVE, lambda e: e.tensor_copy(scvb[:, 0:3], PS[:, 3587:3590]),
        waits=[W(cPE, None)])
    P.c("tensor", cPE, lambda e: e.matmul(PS[:, 3584:3587], ones32[:, :], sh0[0:1, 0:3],
                                          start=True, stop=True), waits=[W(cVE, router_ve)])
    P.c("vector", cVE, lambda e: e.tensor_copy(hbc[0][:, 0:3], PS[:, 3584:3587]),
        waits=[W(cPE, None)])
    P.c("tensor", cPE, lambda e: e.matmul(PS[:, 3584:3587], ones32[:, :], sh1[0:1, 0:3],
                                          start=True, stop=True), waits=[W(cVE, None)])
    P.c("vector", cVE, lambda e: e.tensor_copy(hbc[1][:, 0:3], PS[:, 3584:3587]),
        waits=[W(cPE, None)])
    for i in range(2):
        P.c("vector", cVE, lambda e, i=i: e.tensor_scalar(
            b2c[i][:, :], sb2t[:, 0:32], hbc[i][:, 0:1], None, ALU.mult))
        P.c("vector", cVE, lambda e, i=i: e.scalar_tensor_tensor(
            b2c[i][:, :], sb2t[:, 32:64], hbc[i][:, 1:2], b2c[i][:, :], ALU.mult, ALU.add))
        P.c("vector", cVE, lambda e, i=i: e.scalar_tensor_tensor(
            b2c[i][:, :], sb2t[:, 64:96], hbc[i][:, 2:3], b2c[i][:, :], ALU.mult, ALU.add))
    router_done = cVE.n

    dbg_dump("dbg_pay", lambda: spay[0:1, :], [W(cVE, router_done)])
    dbg_dump("dbg_g0", lambda: sg0[0:1, :], [W(cVE, router_done)])
    dbg_dump("dbg_h01", lambda: sh0[0:1, :], [W(cVE, router_done)])

    P.phase = 'p3'
    last_w1 = {}
    for sh in range(2):
        if sh == 1:
            for fc in range(32):
                P.dma("sync", dT, lambda e, f=fc: e.dma_start(
                    out=shT[:, f * 512:(f + 1) * 512],
                    in_=hscr.ap()[512:1024, f * 128:(f + 1) * 128], transpose=True),
                    [W(dHW, hs_written[7]), W(cVE, None), W(cPE, None)])
            sh1_shT = dT.n
            P.c("tensor", cPE, gatemm, waits=[W(dT, sh1_shT), W(cVE, None)])
            P.c("vector", cVE, lambda e: e.tensor_copy(sglog[0:3, :], PS[0:3, 3072:3584]),
                waits=[W(cPE, None)])

        for sj in range(4):
            P.c("tensor", cPE, lambda e, k=sj: e.transpose(
                PS[:, 3072:3075], sglog[0:3, k * 128:(k + 1) * 128], eye32[0:3, 0:3]),
                waits=[W(cVE, None)])
            P.c("vector", cVE, lambda e, k=sj: e.tensor_copy(sgt[k][:, 0:3], PS[:, 3072:3075]),
                waits=[W(cPE, None)])

            P.c("vector", cVE, lambda e, k=sj: e.tensor_tensor(
                sgt[k][:, 0:3], sgt[k][:, 0:3], scvb[:, 0:3], ALU.add),
                waits=[W(cVE, router_done)])
            P.c("vector", cVE, lambda e, k=sj: e.reduce_max(
                sgs1[:, :], sgt[k][:, 0:3], AX.X, negate=True))
            P.c("scalar", cSC, lambda e, k=sj: e.activation(
                sgx[:, 0:3], sgt[k][:, 0:3], AF.Exp, bias=sgs1[:, 0:1]), waits=[W(cVE, None)])

            for gfn in [
                lambda e: e.reduce_sum(sgs2[:, :], sgx[:, 0:3], AX.X),
                lambda e: e.reciprocal(sgs1[:, :], sgs2[:, :]),
                lambda e: e.tensor_scalar(sgx[:, 0:3], sgx[:, 0:3], sgs1[:, 0:1], None, ALU.mult),
                lambda e: e.reduce_max(sgs1[:, :], sgx[:, 0:3], AX.X),
                lambda e: e.tensor_reduce(sgs2[:, :], sgx[:, 0:3], AX.X, ALU.min),
                lambda e: e.reduce_sum(sgs3[:, :], sgx[:, 0:3], AX.X),
                lambda e: e.tensor_tensor(sgs3[:, :], sgs3[:, :], sgs1[:, :], ALU.subtract),
                lambda e: e.tensor_tensor(sgs3[:, :], sgs3[:, :], sgs2[:, :], ALU.subtract),
                lambda e: e.tensor_tensor(sa1[:, :], sgs1[:, :], sgs3[:, :], ALU.add),
                lambda e: e.tensor_scalar(sa1[:, :], sa1[:, :], 1.0, 1e-8, ALU.mult, ALU.add),
                lambda e: e.reciprocal(sa0[:, :], sa1[:, :]),
                lambda e: e.tensor_tensor(sa1[:, :], sgs3[:, :], sa0[:, :], ALU.mult),
                lambda e: e.tensor_tensor(sa0[:, :], sgs1[:, :], sa0[:, :], ALU.mult),
            ]:
                P.c("vector", cVE, gfn, waits=[W(cSC, None)])
            P.c("tensor", cPE, lambda e: e.transpose(PS[0:1, 3072:3200], sa0[:, 0:1],
                                                     eye32[:, :]), waits=[W(cVE, None)])
            P.c("vector", cVE, lambda e, k=sj: e.tensor_copy(
                a0row[0:1, k * 128:(k + 1) * 128], PS[0:1, 3072:3200]), waits=[W(cPE, None)])
            P.c("tensor", cPE, lambda e: e.transpose(PS[0:1, 3072:3200], sa1[:, 0:1],
                                                     eye32[:, :]), waits=[W(cVE, None)])
            P.c("vector", cVE, lambda e, k=sj: e.tensor_copy(
                a1row[0:1, k * 128:(k + 1) * 128], PS[0:1, 3072:3200]), waits=[W(cPE, None)])
        P.c("tensor", cPE, lambda e: e.matmul(ps(3584, 4096), ones32[:, :], a0row[0:1, :],
                                              start=True, stop=True), waits=[W(cVE, None)])
        P.c("vector", cVE, lambda e: e.tensor_copy(a0bc[:, :], ps(3584, 4096)),
            waits=[W(cPE, None)])
        P.c("tensor", cPE, lambda e: e.matmul(ps(3584, 4096), ones32[:, :], a1row[0:1, :],
                                              start=True, stop=True), waits=[W(cVE, None)])
        P.c("vector", cVE, lambda e: e.tensor_copy(a1bc[:, :], ps(3584, 4096)),
            waits=[W(cPE, None)])
        for ee in range(3):
            P.c("vector", cVE, lambda e, x=ee: e.tensor_scalar(
                sce[x][:, :], a0bc[:, :], hbc[0][:, x:x + 1], None, ALU.mult))
            P.c("vector", cVE, lambda e, x=ee: e.scalar_tensor_tensor(
                sce[x][:, :], a1bc[:, :], hbc[1][:, x:x + 1], sce[x][:, :], ALU.mult, ALU.add))
        gate_fin = cVE.n

        if sh == 0:
            dbg_dump("dbg_gate", lambda: sgt[0][:, :], [W(cVE, gate_fin)])
            dbg_dump("dbg_a0", lambda: a0row[0:1, :], [W(cVE, gate_fin)])
            dbg_dump("dbg_sce", lambda: sce[0][:, :], [W(cVE, gate_fin)])

        P.phase = 'w1_%d' % sh
        last_ve_act = 0
        for ee in range(3):
            fn = [AF.Tanh, AF.Gelu, AF.Silu][ee]
            for gi in range(32):
                slot = (ee * 32 + gi) % 3
                P.dma("gpsimd", dW[slot], lambda e, s=slot, x=ee, g=gi: e.dma_start(
                    out=wpan[s][:, :], in_=w1_h.ap()[x, g, :, :]), [W(cPE, slot_pe[slot])])

                def w1mm(e, s=slot, b=(gi % 2) * 512):
                    r = []
                    for fi in range(32):
                        r.append(e.matmul(ps(b, b + 512), wpan[s][:, fi * 128:(fi + 1) * 128],
                                          shT[:, fi * 512:(fi + 1) * 512],
                                          start=(fi == 0), stop=(fi == 31)))
                    return r
                w = [W(dW[slot], None), W(cVE, gate_fin)]
                key = (ee, gi - 2) if gi >= 2 else (ee - 1, 30 + gi)
                if key in last_w1:
                    w.append(W(cVE, last_w1[key]))
                P.c("tensor", cPE, w1mm, waits=w)
                slot_pe[slot] = cPE.n
                P.c("scalar", cSC, lambda e, b=(gi % 2) * 512, x=ee, g=gi, f=fn: e.activation(
                    sactf[:, :], ps(b, b + 512), f, bias=sb1t[:, x * 32 + g:x * 32 + g + 1]),
                    waits=[W(cPE, None), W(cVE, last_ve_act)])
                va_w = [W(cSC, None)]
                if ee == 2 and gi == 0:
                    va_w.append(W(dHW, hs_written[7]))
                    va_w.append(W(dX2W, x2_written[7]))
                P.c("vector", cVE, lambda e, x=ee, g=gi: e.tensor_tensor(
                    act[x][:, g * 512:(g + 1) * 512], sactf[:, :], sce[x][:, :], ALU.mult),
                    waits=va_w)
                last_ve_act = cVE.n
                last_w1[(ee, gi)] = cVE.n
        act_done = cVE.n

        P.phase = 'w2_%d' % sh
        ve_moe = []
        moe_w = []
        for oi in range(32):
            for ee in range(3):
                slot = (oi * 3 + ee) % 3
                P.dma("gpsimd", dW[slot], lambda e, s=slot, x=ee, o=oi: e.dma_start(
                    out=wpan[s][:, :], in_=w2_h.ap()[x, o, :, :]), [W(cPE, slot_pe[slot])])

                def w2mm(e, s=slot, x=ee, b=2048 + (oi % 2) * 512):
                    r = []
                    for gi in range(32):
                        r.append(e.matmul(ps(b, b + 512), wpan[s][:, gi * 128:(gi + 1) * 128],
                                          act[x][:, gi * 512:(gi + 1) * 512],
                                          start=(x == 0 and gi == 0), stop=(x == 2 and gi == 31)))
                    return r
                w = [W(dW[slot], None), W(cVE, act_done)]
                if oi >= 2 and ee == 0:
                    w.append(W(cVE, ve_moe[oi - 2]))
                P.c("tensor", cPE, w2mm, waits=w)
                slot_pe[slot] = cPE.n

            w = [W(cPE, None)]
            if moe_w:
                w.append(W(dMW, moe_w[-1]))
            P.c("vector", cVE, lambda e, o=oi, b=2048 + (oi % 2) * 512: e.scalar_tensor_tensor(
                smoec[:, :], a0bc[:, :], b2c[0][:, o:o + 1],
                ps(b, b + 512), ALU.mult, ALU.add), waits=w)
            P.c("vector", cVE, lambda e, o=oi: e.scalar_tensor_tensor(
                smoec[:, :], a1bc[:, :], b2c[1][:, o:o + 1], smoec[:, :], ALU.mult, ALU.add))
            ve_moe.append(cVE.n)
            P.dma("sync", dMW, lambda e, o=oi, half=sh: e.dma_start(
                out=moescr.ap()[o * 128:(o + 1) * 128, half * 512:half * 512 + 512],
                in_=smoec[:, :]), [W(cVE, None)])
            moe_w.append(dMW.n)
        moe_written = dMW.n

        P.phase = 'p4_%d' % sh
        ve_fin = []
        for oi in range(32):
            w = [W(dMW, moe_written)]
            if ve_fin:
                w.append(W(cVE, ve_fin[-1]))
            P.dma("sync", dMS, lambda e, o=oi, half=sh: e.dma_start(
                out=mstg[:, :],
                in_=moescr.ap()[o * 128:(o + 1) * 128, half * 512:half * 512 + 512]), w)
            def p4tr(e):
                r = []
                for k in range(4):
                    r.append(e.transpose(PS[:, 3072 + k * 128:3072 + k * 128 + 128],
                                         mstg[:, k * 128:(k + 1) * 128], eye32[:, :]))
                return r
            w = [W(dMS, None)]
            if ve_fin:
                w.append(W(cVE, ve_fin[-1]))
            P.c("tensor", cPE, p4tr, waits=w)
            P.c("vector", cVE, lambda e, o=oi: e.tensor_copy(
                smtmall[:, :].rearrange("p (k f) -> p k f", k=4)[:, :, o * 128:(o + 1) * 128],
                PS[:, 3072:3584].rearrange("p (k f) -> p k f", k=4)), waits=[W(cPE, None)])
            ve_fin.append(cVE.n)
        fin_pieces = cVE.n
        for sj in range(4):
            st = sh * 4 + sj
            ln_stats((lambda k=sj: smtm[k][:, :]), waits=[W(cVE, fin_pieces)])
            P.c("vector", cVE, lambda e, k=sj: e.tensor_scalar(
                smtm[k][:, :], smtm[k][:, :], snm[:, 0:1], srs[:, 0:1], ALU.add, ALU.mult),
                waits=[W(cVE, None)])
            w = [W(dX2W, x2_written[7]), W(cVE, None)]
            if dOUT.n:
                w.append(W(dOUT, dOUT.n))
            P.dma("sync", dX2, lambda e, s=st: e.dma_start(
                out=x2t[:, :], in_=x2scr.ap()[s * 128:(s + 1) * 128, :]), w)
            P.c("vector", cVE, lambda e, k=sj: e.tensor_tensor(
                x2t[:, :], smtm[k][:, :], x2t[:, :], ALU.add), waits=[W(dX2, None)])
            P.dma("sync", dOUT, lambda e, s=st: e.dma_start(
                out=out_h.ap()[s * 128:(s + 1) * 128, :], in_=x2t[:, :]), [W(cVE, None)])

    P.raw("sync", lambda e: e.wait_ge(dOUT.sem, dOUT.n))

    with nc.Block() as block:
        P.replay(block)
    return nc


def kernel(L0_attn_hidden_states, L0_attn_Wq, L0_attn_Wo, L0_attn_kv_cache,
           L0_attn_block_table, L0_ffn_gate_w, L0_ffn_gate_b, L0_ffn_w1,
           L0_ffn_b1, L0_ffn_w2, L0_ffn_b2, _dbg=(), _nc=None, _raw=False):
    x = np.ascontiguousarray(np.asarray(L0_attn_hidden_states, dtype=np.float32).reshape(8192, 4096))
    Wq = np.asarray(L0_attn_Wq, np.float32)
    Wo = np.asarray(L0_attn_Wo, np.float32)
    w1 = np.asarray(L0_ffn_w1, np.float32)
    w2 = np.asarray(L0_ffn_w2, np.float32)
    kv = np.ascontiguousarray(np.asarray(L0_attn_kv_cache, np.float32))
    bt = np.ascontiguousarray(np.asarray(L0_attn_block_table, np.int32))
    gwf = np.asarray(L0_ffn_gate_w, np.float32)
    gbf = np.asarray(L0_ffn_gate_b, np.float32)
    b1 = np.asarray(L0_ffn_b1, np.float32)
    b2 = np.asarray(L0_ffn_b2, np.float32)

    wq_t, wo_t, w1_t, w2_t = _prep_weights(Wq, Wo, w1, w2)
    gw_t = np.ascontiguousarray(gwf[:4096].reshape(32, 128, 3))
    gwl = np.zeros((1, 4), np.float32)
    gwl[0, :3] = gwf[4096]
    gb = np.zeros((1, 4), np.float32)
    gb[0, :3] = gbf
    b1t = np.ascontiguousarray(b1.reshape(3, 32, 128).transpose(2, 0, 1).reshape(128, 96))
    b2t = np.ascontiguousarray(b2.reshape(3, 32, 128).transpose(2, 0, 1).reshape(128, 96))

    tag = np.array([[np.random.randint(1, 2 ** 30)]], np.int32)
    in_maps = []
    for c in range(NCORES):
        half = c % 2
        cos, sin = _rope_tables(half)
        mv, mt = _masks(half)
        in_maps.append({
            "x": x[c * 1024:(c + 1) * 1024],
            "wq": wq_t, "wo": wo_t, "w1": w1_t, "w2": w2_t,
            "kv": kv, "bt": bt, "gw": gw_t, "gwl": gwl, "gb": gb,
            "b1t": b1t, "b2t": b2t,
            "cost": cos, "sint": sin, "maskv": mv, "maskt": mt,
            "par": np.array([[half]], np.int32),
            "parf": np.array([[float(half)]], np.float32),
            "tag": tag,
        })

    nc = _nc if _nc is not None else build_program(dbg=_dbg)
    res = run_bass_kernel_spmd(nc, in_maps, core_ids=list(range(NCORES)))
    out = np.concatenate([res.results[c]["out"] for c in range(NCORES)], axis=0)
    if _raw:
        return out.reshape(4, 2048, 4096), res
    return out.reshape(4, 2048, 4096)
